# version 01
# baseline (speedup 1.0000x reference)
import numpy as np
import ml_dtypes
from contextlib import ExitStack

import concourse.bass as bass
import concourse.mybir as mybir
from concourse.bass_utils import run_bass_kernel_spmd

F32 = mybir.dt.float32
BF16 = mybir.dt.bfloat16
U8 = mybir.dt.uint8
AF = mybir.ActivationFunctionType
ALU = mybir.AluOpType
AX = mybir.AxisListType

S = 4096
D = 1024
NT = S // 128
EPS = 1e-5

BF = ml_dtypes.bfloat16



class Ctr:

    def __init__(self, nc, ctx, name):
        self.sem = ctx.enter_context(nc.semaphore(name))
        self.val = 0

    def inc(self, inst, n=1):
        inst.then_inc(self.sem, n)
        self.val += n
        return self.val


class B:

    def __init__(self, nc, ctx, block):
        self.nc = nc
        self.ctx = ctx
        self.block = block
        self.c = {}
        self._last_wait = {}

    def ctr(self, name):
        if name not in self.c:
            self.c[name] = Ctr(self.nc, self.ctx, name)
        return self.c[name]

    def on(self, engine, fn):
        getattr(self.block, engine)(lambda eng: fn(W(self, engine, eng)))


class W:

    def __init__(self, b, name, eng):
        self.b = b
        self.name = name
        self.eng = eng

    def wait(self, ctr, val=None):
        v = ctr.val if val is None else val
        if v <= 0:
            return
        key = (self.name, id(ctr))
        if self.b._last_wait.get(key, -1) >= v:
            return
        self.eng.wait_ge(ctr.sem, v)
        self.b._last_wait[key] = v

    def __getattr__(self, item):
        return getattr(self.eng, item)



def build_ffn1(nc, names):
    x3 = nc.dram_tensor(names["x"], [S, D], F32, kind="ExternalInput").ap()
    k0 = nc.dram_tensor(names["k0"], [2 * D, 4096], U8, kind="ExternalInput").ap()
    k1 = nc.dram_tensor(names["k1"], [4096, 4096], U8, kind="ExternalInput").ap()
    k2 = nc.dram_tensor(names["k2"], [4096, 1024], U8, kind="ExternalInput").ap()
    s0 = nc.dram_tensor(names["s0"], [1, 1], F32, kind="ExternalInput").ap()
    s1 = nc.dram_tensor(names["s1"], [1, 1], F32, kind="ExternalInput").ap()
    s2 = nc.dram_tensor(names["s2"], [1, 1], F32, kind="ExternalInput").ap()
    out = nc.dram_tensor(names["out"], [S, D], F32, kind="ExternalOutput").ap()
    w0p = nc.dram_tensor("w0p_scratch", [8, 128, 4096], BF16, kind="Internal")
    dbg = None
    if names.get("dbg"):
        dbg = {k: nc.dram_tensor("dbg_" + k, shp, dt, kind="ExternalOutput").ap()
               for k, shp, dt in [("ln3T", [128, 8 * 512], BF16), ("r1", [128, 32 * 512], BF16),
                                  ("r2", [128, 32 * 512], BF16), ("cs1", [1, 512], F32),
                                  ("cs2", [1, 512], F32), ("w0p", [8, 128, 4096], BF16)]}

    ident_np = np.eye(128, dtype=BF)

    with ExitStack() as ctx:
        e = ctx.enter_context
        ident = e(nc.sbuf_tensor("ident", [128, 128], BF16))
        onesf = e(nc.sbuf_tensor("onesf", [1, 128], F32))
        onecol_bf = e(nc.sbuf_tensor("onecol_bf", [128, 1], BF16))
        nhcol_bf = e(nc.sbuf_tensor("nhcol_bf", [1, 128], F32))
        nhrow_bf = e(nc.sbuf_tensor("nhrow_bf", [1, 1024], F32))
        svals = e(nc.sbuf_tensor("svals", [1, 3], F32))
        vec2s0 = e(nc.sbuf_tensor("vec2s0", [128, 1], F32))
        vec4s1 = e(nc.sbuf_tensor("vec4s1", [128, 1], F32))
        vec4s2 = e(nc.sbuf_tensor("vec4s2", [128, 1], F32))
        w0a = e(nc.sbuf_tensor("w0a", [128, 4096], BF16))
        w0b = e(nc.sbuf_tensor("w0b", [128, 4096], BF16))
        w0d = e(nc.sbuf_tensor("w0d", [128, 4096], BF16))
        x3c = e(nc.sbuf_tensor("x3c", [128, 4 * 1024], F32))
        stat = e(nc.sbuf_tensor("stat", [128, 12], F32))
        mv = e(nc.sbuf_tensor("mv", [128, 2], F32))
        rstd = e(nc.sbuf_tensor("rstd", [128, 1], F32))
        lnt = e(nc.sbuf_tensor("lnt", [128, 1024], BF16))
        ln3T = e(nc.sbuf_tensor("ln3T", [128, 8 * 512], BF16))
        r1 = e(nc.sbuf_tensor("r1", [128, 32 * 512], BF16))
        r2 = e(nc.sbuf_tensor("r2", [128, 32 * 512], BF16))
        wblk = [e(nc.sbuf_tensor(f"wblk{i}", [128, 1024], BF16)) for i in range(2)]
        csum1 = e(nc.sbuf_tensor("csum1", [1, 512], F32))
        csum2 = e(nc.sbuf_tensor("csum2", [1, 512], F32))
        outb = [e(nc.sbuf_tensor(f"outb{i}", [128, 1024], F32)) for i in range(2)]

        block = e(nc.Block())
        b = B(nc, ctx, block)
        sl = b.ctr("sl")
        sg = b.ctr("sg")
        so = b.ctr("so")
        pe = b.ctr("pe")
        dv = b.ctr("dv")
        sc = b.ctr("sc")

        identh = nc.inline_tensor(ident_np, "identnp")

        def p0_sync(s):
            sl.inc(s.dma_start(out=ident[:, :], in_=identh.ap()), 16)
            sl.inc(s.dma_start(out=svals[0:1, 0:1], in_=s0), 16)
            sl.inc(s.dma_start(out=svals[0:1, 1:2], in_=s1), 16)
            sl.inc(s.dma_start(out=svals[0:1, 2:3], in_=s2), 16)
        b.on("sync", p0_sync)

        def p0_vec(v):
            v.memset(onesf[:, :], 1.0)
            v.memset(onecol_bf[:, :], 1.0)
            v.memset(nhcol_bf[:, :], -0.5)
            dv.inc(v.memset(nhrow_bf[:, :], -0.5))
        b.on("vector", p0_vec)

        with nc.psum_tensor("pb0", [128, 512], F32) as pb0:
            for i, (src, dst, m) in enumerate([(0, vec2s0, 2.0), (1, vec4s1, 4.0), (2, vec4s2, 4.0)]):
                def t_body(t, i=i, src=src):
                    t.wait(sl)
                    t.wait(dv)
                    pe.inc(t.matmul(pb0[:, 0:1], onesf[0:1, :], svals[0:1, src:src + 1],
                                    start=True, stop=True))
                b.on("tensor", t_body)

                def v_body(v, dst=dst, m=m):
                    v.wait(pe)
                    dv.inc(v.tensor_scalar_mul(dst[:, 0:1], pb0[:, 0:1], float(m)))
                b.on("vector", v_body)

        for it in range(8):
            def g_body(g, it=it):
                g.wait(so)
                sg.inc(g.dma_start(out=w0a[:, :], in_=k0[it * 128:(it + 1) * 128, :]), 16)
                sg.inc(g.dma_start(out=w0b[:, :], in_=k0[1024 + it * 128:1024 + (it + 1) * 128, :]), 16)
            b.on("gpsimd", g_body)

            def v_body(v):
                v.wait(sg)
                v.wait(so)
                dv.inc(v.tensor_sub(w0d[:, :], w0a[:, :], w0b[:, :]))
            b.on("vector", v_body)

            def s_body(s, it=it):
                s.wait(dv)
                so.inc(s.dma_start(out=w0p.ap()[it, :, :], in_=w0d[:, :]), 16)
            b.on("sync", s_body)

        for c in range(8):
            for st in range(4):
                gt = c * 4 + st

                def s_body(s, st=st, gt=gt):
                    s.wait(dv)
                    sl.inc(s.dma_start(out=x3c[:, st * 1024:(st + 1) * 1024],
                                       in_=x3[gt * 128:(gt + 1) * 128, :]), 16)
                b.on("sync", s_body)

                def v_body(v, st=st):
                    v.wait(sl)
                    v.wait(dv)
                    v.bn_stats(stat[:, 0:6], x3c[:, st * 1024:st * 1024 + 512])
                    dv.inc(v.bn_stats(stat[:, 6:12], x3c[:, st * 1024 + 512:st * 1024 + 1024]))
                    v.wait(dv)
                    dv.inc(v.bn_aggr(mv[:, :], stat[:, :]))
                    v.wait(dv)
                    dv.inc(v.tensor_scalar_add(rstd[:, 0:1], mv[:, 1:2], EPS))
                b.on("vector", v_body)

                def s2_body(s):
                    s.wait(dv)
                    sc.inc(s.activation(rstd[:, 0:1], rstd[:, 0:1], AF.Sqrt))
                b.on("scalar", s2_body)

                def v2_body(v, st=st):
                    v.wait(sc)
                    dv.inc(v.reciprocal(rstd[:, 0:1], rstd[:, 0:1]))
                    v.wait(dv)
                    dv.inc(v.tensor_scalar(lnt[:, :], x3c[:, st * 1024:(st + 1) * 1024],
                                           mv[:, 0:1], rstd[:, 0:1], ALU.subtract, ALU.mult))
                b.on("vector", v2_body)

                with nc.psum_tensor(f"ptr_{c}_{st}", [128, 1024], BF16) as ptr:
                    def t_body(t, st=st):
                        t.wait(dv)
                        t.wait(sl)
                        for it in range(8):
                            i = pe.inc(t.transpose(ptr[:, it * 128:(it + 1) * 128],
                                                   lnt[:, it * 128:(it + 1) * 128], ident[:, :]))
                    b.on("tensor", t_body)

                    def v3_body(v, st=st):
                        v.wait(pe)
                        for it in range(8):
                            i = v.tensor_copy(ln3T[:, it * 512 + st * 128:it * 512 + (st + 1) * 128],
                                              ptr[:, it * 128:(it + 1) * 128])
                        dv.inc(i)
                    b.on("vector", v3_body)

            with ExitStack() as sctx:
                pg = [sctx.enter_context(nc.psum_tensor(f"pg_{c}_{j}", [128, 512], F32))
                      for j in range(8)]
                for mg in range(4):
                    for it in range(8):
                        buf = wblk[it % 2]

                        def g_body(g, mg=mg, it=it, buf=buf):
                            if mg + it > 0:
                                g.wait(pe)
                            sg.inc(g.dma_start(out=buf[:, :],
                                               in_=w0p.ap()[it, :, mg * 1024:(mg + 1) * 1024]), 16)
                        b.on("gpsimd", g_body)

                        def t_body(t, mg=mg, it=it, buf=buf):
                            t.wait(sg)
                            if it == 0:
                                t.wait(sc)
                            if mg == 0 and it == 0:
                                t.wait(dv)
                            for j in range(8):
                                pe.inc(t.matmul(pg[j][:, :], buf[:, j * 128:(j + 1) * 128],
                                                ln3T[:, it * 512:(it + 1) * 512],
                                                start=(it == 0), stop=(it == 7)))
                        b.on("tensor", t_body)

                    def s_body(s, mg=mg):
                        s.wait(pe)
                        s.wait(dv)
                        for j in range(8):
                            sc.inc(s.activation(r1[:, (mg * 8 + j) * 512:(mg * 8 + j + 1) * 512],
                                                pg[j][:, :], AF.Relu, scale=vec2s0[:, 0:1]))
                    b.on("scalar", s_body)

            if dbg is not None and c == 0:
                def dmp1(s):
                    s.wait(sc)
                    s.wait(dv)
                    so.inc(s.dma_start(out=dbg["ln3T"], in_=ln3T[:, :]), 16)
                    so.inc(s.dma_start(out=dbg["r1"], in_=r1[:, :]), 16)
                b.on("sync", dmp1)

            with ExitStack() as sctx:
                with nc.psum_tensor(f"pcs_{c}", [1, 512], F32) as pcs:
                    def t_body(t, pcs=pcs):
                        t.wait(sc)
                        for kb in range(32):
                            pe.inc(t.matmul(pcs[:, :], onecol_bf[:, 0:1], r1[:, kb * 512:(kb + 1) * 512],
                                            start=(kb == 0), stop=(kb == 31)))
                    b.on("tensor", t_body)

                    def v_body(v, pcs=pcs):
                        v.wait(pe)
                        dv.inc(v.tensor_copy(csum1[0:1, :], pcs[0:1, :]))
                    b.on("vector", v_body)

                pg = [sctx.enter_context(nc.psum_tensor(f"ph_{c}_{j}", [128, 512], F32))
                      for j in range(8)]
                for mg in range(4):
                    for kb in range(32):
                        buf = wblk[kb % 2]

                        def g_body(g, mg=mg, kb=kb, buf=buf):
                            g.wait(pe)
                            sg.inc(g.dma_start(out=buf[:, :],
                                               in_=k1[kb * 128:(kb + 1) * 128, mg * 1024:(mg + 1) * 1024]), 16)
                        b.on("gpsimd", g_body)

                        def t_body(t, mg=mg, kb=kb, buf=buf):
                            t.wait(sg)
                            if kb == 0:
                                t.wait(sc)
                                t.wait(dv)
                            for j in range(8):
                                pe.inc(t.matmul(pg[j][:, :], buf[:, j * 128:(j + 1) * 128],
                                                r1[:, kb * 512:(kb + 1) * 512],
                                                start=(kb == 0), stop=False))
                        b.on("tensor", t_body)

                    def t2_body(t, mg=mg):
                        for j in range(8):
                            pe.inc(t.matmul(pg[j][:, :], nhcol_bf[0:1, :], csum1[0:1, :],
                                            start=False, stop=True))
                    b.on("tensor", t2_body)

                    def s_body(s, mg=mg):
                        s.wait(pe)
                        for j in range(8):
                            sc.inc(s.activation(r2[:, (mg * 8 + j) * 512:(mg * 8 + j + 1) * 512],
                                                pg[j][:, :], AF.Relu, scale=vec4s1[:, 0:1]))
                    b.on("scalar", s_body)

            if dbg is not None and c == 0:
                def dmp2(s):
                    s.wait(sc)
                    s.wait(dv)
                    so.inc(s.dma_start(out=dbg["r2"], in_=r2[:, :]), 16)
                    so.inc(s.dma_start(out=dbg["cs1"], in_=csum1[0:1, :]), 16)
                b.on("sync", dmp2)

            with ExitStack() as sctx:
                with nc.psum_tensor(f"pcs2_{c}", [1, 512], F32) as pcs2:
                    def t_body(t, pcs2=pcs2):
                        t.wait(sc)
                        for tt in range(32):
                            pe.inc(t.matmul(pcs2[:, :], onecol_bf[:, 0:1], r2[:, tt * 512:(tt + 1) * 512],
                                            start=(tt == 0), stop=(tt == 31)))
                    b.on("tensor", t_body)

                    def v_body(v, pcs2=pcs2):
                        v.wait(pe)
                        dv.inc(v.tensor_copy(csum2[0:1, :], pcs2[0:1, :]))
                    b.on("vector", v_body)

                if dbg is not None and c == 0:
                    def dmp3(s):
                        s.wait(dv)
                        so.inc(s.dma_start(out=dbg["cs2"], in_=csum2[0:1, :]), 16)
                    b.on("sync", dmp3)

                ps2 = [sctx.enter_context(nc.psum_tensor(f"ps2_{c}_{st}", [128, 1024], F32))
                       for st in range(4)]
                for tt in range(32):
                    buf = wblk[tt % 2]

                    def g_body(g, tt=tt, buf=buf):
                        g.wait(pe)
                        sg.inc(g.dma_start(out=buf[:, :], in_=k2[tt * 128:(tt + 1) * 128, :]), 16)
                    b.on("gpsimd", g_body)

                    def t2_body(t, tt=tt, buf=buf):
                        t.wait(sg)
                        if tt == 0:
                            t.wait(dv)
                        for st in range(4):
                            for n in range(2):
                                pe.inc(t.matmul(ps2[st][:, n * 512:(n + 1) * 512],
                                                r2[:, tt * 512 + st * 128:tt * 512 + (st + 1) * 128],
                                                buf[:, n * 512:(n + 1) * 512],
                                                start=(tt == 0), stop=False))
                    b.on("tensor", t2_body)

                def t3_body(t):
                    for st in range(4):
                        for n in range(2):
                            pe.inc(t.matmul(ps2[st][:, n * 512:(n + 1) * 512],
                                            csum2[0:1, st * 128:(st + 1) * 128],
                                            nhrow_bf[0:1, n * 512:(n + 1) * 512],
                                            start=False, stop=True))
                b.on("tensor", t3_body)

                for st in range(4):
                    ob = outb[st % 2]
                    sob = b.ctr(f"sob{st % 2}")

                    def v2_body(v, st=st, ob=ob, sob=sob):
                        v.wait(pe)
                        v.wait(sob)
                        for n in range(2):
                            dv.inc(v.scalar_tensor_tensor(ob[:, n * 512:(n + 1) * 512],
                                                          ps2[st][:, n * 512:(n + 1) * 512],
                                                          vec4s2[:, 0:1],
                                                          x3c[:, st * 1024 + n * 512:st * 1024 + (n + 1) * 512],
                                                          ALU.mult, ALU.add))
                    b.on("vector", v2_body)

                    def s_body(s, st=st, ob=ob, c=c, sob=sob):
                        s.wait(dv)
                        gt = c * 4 + st
                        sob.inc(s.dma_start(out=out[gt * 128:(gt + 1) * 128, :], in_=ob[:, :]), 16)
                    b.on("sync", s_body)

        if dbg is not None:
            def dump(s):
                s.wait(so)
                so.inc(s.dma_start(out=dbg["w0p"], in_=w0p.ap()), 16)
            b.on("sync", dump)

        def fin(s):
            s.wait(so)
            for name in ("sob0", "sob1"):
                if name in b.c:
                    s.wait(b.c[name])
        b.on("sync", fin)
    return nc



def build_attn0(nc, names):
    x0 = nc.dram_tensor(names["x"], [S, D], F32, kind="ExternalInput").ap()
    Wd = {w: nc.dram_tensor(names[w], [D, D], F32, kind="ExternalInput").ap()
          for w in ("wq", "wk", "wv", "wo")}
    out = nc.dram_tensor(names["out"], [S, D], F32, kind="ExternalOutput").ap()
    ln0h = nc.dram_tensor("ln0T_scratch", [8, 128, S], BF16, kind="Internal")
    yph = nc.dram_tensor("yp_scratch", [8, 128, S], F32, kind="Internal")
    dbg = None
    if names.get("dbg"):
        dbg = {k: nc.dram_tensor("dbg_" + k, shp, dt, kind="ExternalOutput").ap()
               for k, shp, dt in [("KTg", [128, 4 * 4096], BF16), ("Vg", [128, 32 * 520], BF16),
                                  ("qcb", [128, 4 * 512], BF16), ("attnoT", [128, 4 * 512], BF16),
                                  ("ab0", [128, 512], BF16), ("pv0", [65, 512], F32),
                                  ("WTwk", [128, 8192], BF16), ("ln0", [8, 128, 4096], BF16)]}

    masks_np = np.zeros((128, 4 * 512), dtype=BF)
    for p in range(4):
        for kl in range(128):
            q0 = kl + 128 * p
            if q0 < 512:
                masks_np[kl, p * 512 + q0:(p + 1) * 512] = 1.0

    with ExitStack() as ctx:
        e = ctx.enter_context
        ident = e(nc.sbuf_tensor("ident", [128, 128], BF16))
        identf = e(nc.sbuf_tensor("identf", [128, 128], F32))
        onesf = e(nc.sbuf_tensor("onesf", [128, 64], F32))
        masks = e(nc.sbuf_tensor("masks", [128, 4 * 512], BF16))
        WT = {w: e(nc.sbuf_tensor(f"WT{w}", [128, 8192], BF16)) for w in ("wq", "wk", "wv", "wo")}
        x0t = [e(nc.sbuf_tensor(f"x0t{i}", [128, 1024], F32)) for i in range(2)]
        stat = e(nc.sbuf_tensor("stat", [128, 12], F32))
        mv = e(nc.sbuf_tensor("mv", [128, 2], F32))
        rstd = e(nc.sbuf_tensor("rstd", [128, 1], F32))
        lnt = e(nc.sbuf_tensor("lnt", [128, 1024], BF16))

        block = e(nc.Block())
        b = B(nc, ctx, block)
        sl = b.ctr("sl"); sg = b.ctr("sg"); pe = b.ctr("pe")
        dv = b.ctr("dv"); sc = b.ctr("sc"); sp = b.ctr("sp"); sy = b.ctr("sy")
        sob = [b.ctr("sob0"), b.ctr("sob1")]

        identh = nc.inline_tensor(np.eye(128, dtype=BF), "identnp")
        identfh = nc.inline_tensor(np.eye(128, dtype=np.float32), "identfnp")
        masksh = nc.inline_tensor(masks_np, "masksnp")

        def a0_sync(s):
            sl.inc(s.dma_start(out=ident[:, :], in_=identh.ap()), 16)
            sl.inc(s.dma_start(out=identf[:, :], in_=identfh.ap()), 16)
            sl.inc(s.dma_start(out=masks[:, :], in_=masksh.ap()), 16)
        b.on("sync", a0_sync)

        def a0_vec(v):
            dv.inc(v.memset(onesf[:, :], 1.0))
        b.on("vector", a0_vec)

        with ExitStack() as actx:
            wstage = actx.enter_context(nc.sbuf_tensor("wstage", [128, 8192], BF16))
            ln0T = actx.enter_context(nc.sbuf_tensor("ln0T", [128, 8 * 4096], BF16))
            for wi, w in enumerate(("wq", "wk", "wv", "wo")):
                def g_body(g, w=w, wi=wi):
                    if wi > 0:
                        g.wait(pe)
                    sg.inc(g.dma_start(out=wstage[:, :].rearrange("p (t i) -> p t i", t=8),
                                       in_=Wd[w].rearrange("(t p) i -> p t i", p=128)), 16)
                b.on("gpsimd", g_body)

                with nc.psum_tensor(f"ptw_{w}", [128, 1024], BF16) as ptw:
                    for u in range(8):
                        def t_body(t, w=w, u=u):
                            t.wait(sg)
                            t.wait(sl)
                            if u > 0:
                                t.wait(dv)
                            for tt in range(8):
                                pe.inc(t.transpose(ptw[:, tt * 128:(tt + 1) * 128],
                                                   wstage[:, tt * 1024 + u * 128:tt * 1024 + (u + 1) * 128],
                                                   ident[:, :]))
                        b.on("tensor", t_body)

                        def v_body(v, w=w, u=u):
                            v.wait(pe)
                            dv.inc(v.tensor_copy(WT[w][:, u * 1024:(u + 1) * 1024], ptw[:, :]))
                        b.on("vector", v_body)

            for st in range(NT):
                xb = x0t[st % 2]

                def s_body(s, st=st, xb=xb):
                    s.wait(dv)
                    sl.inc(s.dma_start(out=xb[:, :], in_=x0[st * 128:(st + 1) * 128, :]), 16)
                b.on("sync", s_body)

                def v_body(v, st=st, xb=xb):
                    v.wait(sl)
                    v.wait(dv)
                    v.bn_stats(stat[:, 0:6], xb[:, 0:512])
                    dv.inc(v.bn_stats(stat[:, 6:12], xb[:, 512:1024]))
                    v.wait(dv)
                    dv.inc(v.bn_aggr(mv[:, :], stat[:, :]))
                    v.wait(dv)
                    dv.inc(v.tensor_scalar_add(rstd[:, 0:1], mv[:, 1:2], EPS))
                b.on("vector", v_body)

                def s2_body(s):
                    s.wait(dv)
                    sc.inc(s.activation(rstd[:, 0:1], rstd[:, 0:1], AF.Sqrt))
                b.on("scalar", s2_body)

                def v2_body(v, st=st, xb=xb):
                    v.wait(sc)
                    dv.inc(v.reciprocal(rstd[:, 0:1], rstd[:, 0:1]))
                    v.wait(dv)
                    dv.inc(v.tensor_scalar(lnt[:, :], xb[:, :], mv[:, 0:1], rstd[:, 0:1],
                                           ALU.subtract, ALU.mult))
                b.on("vector", v2_body)

                with nc.psum_tensor(f"plt_{st}", [128, 1024], BF16) as plt:
                    def t_body(t, st=st):
                        t.wait(dv)
                        for it in range(8):
                            pe.inc(t.transpose(plt[:, it * 128:(it + 1) * 128],
                                               lnt[:, it * 128:(it + 1) * 128], ident[:, :]))
                    b.on("tensor", t_body)

                    def v3_body(v, st=st):
                        v.wait(pe)
                        for it in range(8):
                            i = v.tensor_copy(ln0T[:, it * 4096 + st * 128:it * 4096 + (st + 1) * 128],
                                              plt[:, it * 128:(it + 1) * 128])
                        dv.inc(i)
                    b.on("vector", v3_body)

            def spill(s):
                s.wait(dv)
                for it in range(8):
                    sp.inc(s.dma_start(out=ln0h.ap()[it, :, :], in_=ln0T[:, it * 4096:(it + 1) * 4096]), 16)
            b.on("sync", spill)

        KTg = e(nc.sbuf_tensor("KTg", [128, 4 * 4096], BF16))
        Vg = e(nc.sbuf_tensor("Vg", [128, 32 * 520], BF16))
        lncb = [e(nc.sbuf_tensor(f"lncb{i}", [128, 8 * 512], BF16)) for i in range(2)]
        qcb = e(nc.sbuf_tensor("qcb", [128, 4 * 512], BF16))
        ab = [e(nc.sbuf_tensor(f"ab{i}", [128, 512], BF16)) for i in range(4)]
        brs = e(nc.sbuf_tensor("brs", [128, 512], F32))
        brd = e(nc.sbuf_tensor("brd", [64, 512], F32))
        attnoT = e(nc.sbuf_tensor("attnoT", [128, 4 * 512], BF16))
        ybuf = [e(nc.sbuf_tensor(f"ybuf{i}", [128, 512], F32)) for i in range(2)]
        y0b = [e(nc.sbuf_tensor(f"y0b{i}", [128, 512], F32)) for i in range(2)]

        def ln_chunk_load(s, buf, n, extra=None):
            sl.inc(s.dma_start(out=buf[:, :].rearrange("p (it w) -> p it w", it=8),
                               in_=ln0h.ap()[:, :, n * 512:(n + 1) * 512].rearrange("it p w -> p it w")), 16)

        for gp in range(2):
            def v_init(v, gp=gp):
                v.wait(pe)
                v.wait(sp)
                dv.inc(v.memset(Vg[:, :], 1.0))
            b.on("vector", v_init)

            with ExitStack() as sctx:
                pkv = [sctx.enter_context(nc.psum_tensor(f"pkv_{gp}_{i}", [128, 512], F32)) for i in range(2)]
                for n in range(8):
                    buf = lncb[n % 2]

                    def s_body(s, gp=gp, n=n, buf=buf):
                        s.wait(pe)
                        if gp == 0 and n == 0:
                            s.wait(sp)
                        ln_chunk_load(s, buf, n)
                    b.on("sync", s_body)

                    for j in range(4):
                        def t_body(t, gp=gp, n=n, j=j, buf=buf):
                            t.wait(sl)
                            t.wait(sc)
                            if n == 0 and j == 0:
                                t.wait(dv)
                            for it in range(8):
                                i = t.matmul(pkv[j % 2][:, :],
                                             WT["wk"][:, it * 1024 + gp * 512 + j * 128:it * 1024 + gp * 512 + (j + 1) * 128],
                                             buf[:, it * 512:(it + 1) * 512],
                                             start=(it == 0), stop=(it == 7))
                            pe.inc(i)
                        b.on("tensor", t_body)

                        def sc_body(s, gp=gp, n=n, j=j):
                            s.wait(pe)
                            sc.inc(s.activation(KTg[:, j * 4096 + n * 512:j * 4096 + (n + 1) * 512],
                                                pkv[j % 2][:, :], AF.Identity))
                        b.on("scalar", sc_body)

                    for m4 in range(4):
                        stile = n * 4 + m4

                        def t_body(t, gp=gp, n=n, m4=m4, buf=buf, stile=stile):
                            t.wait(sc)
                            for it in range(8):
                                i = t.matmul(pkv[m4 % 2][:, :],
                                             buf[:, it * 512 + m4 * 128:it * 512 + (m4 + 1) * 128],
                                             WT["wv"][:, it * 1024 + gp * 512:it * 1024 + (gp + 1) * 512],
                                             start=(it == 0), stop=(it == 7))
                            pe.inc(i)
                        b.on("tensor", t_body)

                        def v_body(v, gp=gp, stile=stile, m4=m4):
                            v.wait(pe)
                            dst = bass.AP(Vg, stile * 520, [[32 * 520, 128], [65, 8], [1, 64]])
                            src = bass.AP(pkv[m4 % 2], 0, [[512, 128], [64, 8], [1, 64]])
                            dv.inc(v.tensor_copy(dst, src))
                        b.on("vector", v_body)

            if dbg is not None and gp == 0:
                sdbg = b.ctr("sdbg")

                def d_kv(s):
                    s.wait(dv)
                    s.wait(sc)
                    sdbg.inc(s.dma_start(out=dbg["KTg"], in_=KTg[:, :]), 16)
                    sdbg.inc(s.dma_start(out=dbg["Vg"], in_=Vg[:, :]), 16)
                    sdbg.inc(s.dma_start(out=dbg["WTwk"], in_=WT["wk"][:, :]), 16)
                    sdbg.inc(s.dma_start(out=dbg["ln0"], in_=ln0h.ap()), 16)
                b.on("sync", d_kv)

                def d_bar(v):
                    v.wait(b.c["sdbg"])
                    dv.inc(v.memset(brs[0:1, 0:1], 0.0))
                b.on("vector", d_bar)

            for qc in range(8):
                buf = lncb[qc % 2]

                def s_body(s, gp=gp, qc=qc, buf=buf):
                    s.wait(pe)
                    ln_chunk_load(s, buf, qc)
                b.on("sync", s_body)

                with ExitStack() as sctx:
                    pq = sctx.enter_context(nc.psum_tensor(f"pq_{gp}_{qc}", [128, 512], F32))
                    psab = [sctx.enter_context(nc.psum_tensor(f"ps_{gp}_{qc}_{i}", [128, 512], F32))
                            for i in range(4)]
                    pv = [sctx.enter_context(nc.psum_tensor(f"pv_{gp}_{qc}_{i}", [65, 512], F32))
                          for i in range(2)]
                    pbr = sctx.enter_context(nc.psum_tensor(f"pbr_{gp}_{qc}", [64, 512], F32))

                    for j in range(4):
                        def t_q(t, gp=gp, qc=qc, j=j, buf=buf):
                            t.wait(sl)
                            t.wait(sc)
                            if j == 0:
                                t.wait(dv)
                            for it in range(8):
                                i = t.matmul(pq[:, :],
                                             WT["wq"][:, it * 1024 + gp * 512 + j * 128:it * 1024 + gp * 512 + (j + 1) * 128],
                                             buf[:, it * 512:(it + 1) * 512],
                                             start=(it == 0), stop=(it == 7))
                            pe.inc(i)
                        b.on("tensor", t_q)

                        def sc_q(s, gp=gp, qc=qc, j=j):
                            s.wait(pe)
                            sc.inc(s.activation(qcb[:, j * 512:(j + 1) * 512], pq[:, :], AF.Identity))
                        b.on("scalar", sc_q)

                    nkb = 4 * (qc + 1)
                    for j in range(4):
                        for kb in range(nkb):
                            pA = psab[2 * (kb % 2)]
                            pB = psab[2 * (kb % 2) + 1]
                            aA = ab[2 * (kb % 2)]
                            aB = ab[2 * (kb % 2) + 1]

                            def t_s(t, gp=gp, qc=qc, j=j, kb=kb, pA=pA, pB=pB):
                                t.wait(sc)
                                t.wait(dv)
                                t.matmul(pA[:, :], KTg[0:64, j * 4096 + kb * 128:j * 4096 + (kb + 1) * 128],
                                         qcb[0:64, j * 512:(j + 1) * 512], start=True, stop=True,
                                         tile_position=(0, 0))
                                pe.inc(t.matmul(pB[:, :], KTg[64:128, j * 4096 + kb * 128:j * 4096 + (kb + 1) * 128],
                                                qcb[64:128, j * 512:(j + 1) * 512], start=True, stop=True,
                                                tile_position=(64, 0)))
                            b.on("tensor", t_s)

                            def sc_s(s, gp=gp, qc=qc, j=j, kb=kb, pA=pA, pB=pB, aA=aA, aB=aB):
                                s.wait(pe)
                                s.wait(dv)
                                s.activation(aA[:, :], pA[:, :], AF.Exp, scale=0.125)
                                sc.inc(s.activation(aB[:, :], pB[:, :], AF.Exp, scale=0.125))
                            b.on("scalar", sc_s)

                            diag = (kb >= 4 * qc)
                            if diag:
                                p_ = kb - 4 * qc

                                def v_m(v, aA=aA, aB=aB, p_=p_):
                                    v.wait(sc)
                                    dv.inc(v.tensor_mul(aA[:, :], aA[:, :], masks[:, p_ * 512:(p_ + 1) * 512]))
                                    dv.inc(v.tensor_mul(aB[:, :], aB[:, :], masks[:, p_ * 512:(p_ + 1) * 512]))
                                b.on("vector", v_m)

                            def t_av(t, gp=gp, qc=qc, j=j, kb=kb, aA=aA, aB=aB, diag=diag, nkb=nkb):
                                if diag:
                                    t.wait(dv)
                                else:
                                    t.wait(sc)
                                t.matmul(pv[0][:, :], Vg[:, kb * 520 + 2 * j * 65:kb * 520 + 2 * j * 65 + 65],
                                         aA[:, :], start=(kb == 0), stop=(kb == nkb - 1))
                                pe.inc(t.matmul(pv[1][:, :], Vg[:, kb * 520 + (2 * j + 1) * 65:kb * 520 + (2 * j + 1) * 65 + 65],
                                                aB[:, :], start=(kb == 0), stop=(kb == nkb - 1)))
                            b.on("tensor", t_av)

                        for hh in range(2):
                            def v_n(v, gp=gp, qc=qc, j=j, hh=hh):
                                v.wait(pe)
                                dv.inc(v.reciprocal(brs[64:65, :], pv[hh][64:65, :]))
                            b.on("vector", v_n)

                            def t_n(t, gp=gp, qc=qc, j=j, hh=hh):
                                t.wait(dv)
                                pe.inc(t.matmul(pbr[:, :], onesf[64:65, 0:64], brs[64:65, :], start=True, stop=True))
                            b.on("tensor", t_n)

                            def v_n2(v, gp=gp, qc=qc, j=j, hh=hh):
                                v.wait(pe)
                                dv.inc(v.tensor_copy(brd[:, :], pbr[:, :]))
                                v.wait(dv)
                                base = j * 512
                                if hh == 0:
                                    dv.inc(v.tensor_mul(attnoT[0:64, base:base + 512], pv[0][0:64, :], brd[0:64, :]))
                                else:
                                    dv.inc(v.tensor_mul(attnoT[64:128, base:base + 512], pv[1][0:64, :], brd[0:64, :]))
                            b.on("vector", v_n2)

                    if dbg is not None and gp == 0 and qc == 0:
                        def d_at(s):
                            s.wait(dv)
                            sdbg = b.ctr("sdbg")
                            sdbg.inc(s.dma_start(out=dbg["qcb"], in_=qcb[:, :]), 16)
                            sdbg.inc(s.dma_start(out=dbg["attnoT"], in_=attnoT[:, :]), 16)
                        b.on("sync", d_at)

                        def d_bar2(v):
                            v.wait(b.c["sdbg"])
                            dv.inc(v.memset(brs[0:1, 0:1], 0.0))
                        b.on("vector", d_bar2)

                    for m in range(8):
                        yb = ybuf[m % 2]

                        def t_o(t, gp=gp, qc=qc, m=m):
                            t.wait(dv)
                            t.wait(sc)
                            for itv in range(4):
                                i = t.matmul(pq[:, :],
                                             WT["wo"][:, (4 * gp + itv) * 1024 + m * 128:(4 * gp + itv) * 1024 + (m + 1) * 128],
                                             attnoT[:, itv * 512:(itv + 1) * 512],
                                             start=(itv == 0), stop=(itv == 3))
                            pe.inc(i)
                        b.on("tensor", t_o)

                        if gp == 0:
                            def v_o(v, qc=qc, m=m, yb=yb):
                                v.wait(pe)
                                v.wait(sy)
                                dv.inc(v.tensor_copy(yb[:, :], pq[:, :]))
                            b.on("vector", v_o)

                            def s_o(s, qc=qc, m=m, yb=yb):
                                s.wait(dv)
                                sy.inc(s.dma_start(out=yph.ap()[m, :, qc * 512:(qc + 1) * 512], in_=yb[:, :]), 16)
                            b.on("sync", s_o)
                        else:
                            y0 = y0b[m % 2]

                            def s_o0(s, qc=qc, m=m, y0=y0):
                                s.wait(dv)
                                sy.inc(s.dma_start(out=y0[:, :], in_=yph.ap()[m, :, qc * 512:(qc + 1) * 512]), 16)
                            b.on("sync", s_o0)

                            def v_o(v, qc=qc, m=m, yb=yb, y0=y0):
                                v.wait(pe)
                                v.wait(sy)
                                v.wait(sob[m % 2])
                                dv.inc(v.tensor_add(yb[:, :], pq[:, :], y0[:, :]))
                            b.on("vector", v_o)

                            def s_o(s, qc=qc, m=m, yb=yb):
                                s.wait(dv)
                                sob[m % 2].inc(s.dma_start(out=yph.ap()[m, :, qc * 512:(qc + 1) * 512], in_=yb[:, :]), 16)
                            b.on("sync", s_o)

        ytb = [e(nc.sbuf_tensor(f"ytb{i}", [128, 1024], F32)) for i in range(2)]
        outt = [e(nc.sbuf_tensor(f"outt{i}", [128, 1024], F32)) for i in range(2)]
        soo = [b.ctr("soo0"), b.ctr("soo1")]
        with ExitStack() as sctx:
            pxt = [sctx.enter_context(nc.psum_tensor(f"pxt_{i}", [128, 1024], F32)) for i in range(2)]
            for st in range(NT):
                xb = x0t[st % 2]
                yt = ytb[st % 2]
                ob = outt[st % 2]

                def s_body(s, st=st, xb=xb, yt=yt):
                    s.wait(dv)
                    if st < 2:
                        s.wait(sob[0]); s.wait(sob[1])
                    sl.inc(s.dma_start(out=xb[:, :], in_=x0[st * 128:(st + 1) * 128, :]), 16)
                    sl.inc(s.dma_start(out=yt[:, :].rearrange("p (m w) -> p m w", m=8),
                                       in_=yph.ap()[:, :, st * 128:(st + 1) * 128].rearrange("m p w -> p m w")), 16)
                b.on("sync", s_body)

                def t_body(t, st=st, yt=yt):
                    t.wait(sl)
                    t.wait(dv)
                    for m in range(8):
                        pe.inc(t.transpose(pxt[st % 2][:, m * 128:(m + 1) * 128],
                                           yt[:, m * 128:(m + 1) * 128], identf[:, :]))
                b.on("tensor", t_body)

                def v_body(v, st=st, xb=xb, ob=ob):
                    v.wait(pe)
                    v.wait(soo[st % 2])
                    dv.inc(v.tensor_add(ob[:, :], pxt[st % 2][:, :], xb[:, :]))
                b.on("vector", v_body)

                def s2_body(s, st=st, ob=ob):
                    s.wait(dv)
                    soo[st % 2].inc(s.dma_start(out=out[st * 128:(st + 1) * 128, :], in_=ob[:, :]), 16)
                b.on("sync", s2_body)

        def fin(s):
            s.wait(soo[0]); s.wait(soo[1])
        b.on("sync", fin)
    return nc



def build_moe(nc, names, mode):
    x1 = nc.dram_tensor(names["x"], [S, D], F32, kind="ExternalInput").ap()
    gw = nc.dram_tensor(names["gate_w"], [8, D], F32, kind="ExternalInput").ap()
    lb = nc.dram_tensor(names["lb_bias"], [1, 8], F32, kind="ExternalInput").ap()
    if mode == "full":
        egw = nc.dram_tensor(names["egw"], [8, 512, D], F32, kind="ExternalInput").ap()
        euw = nc.dram_tensor(names["euw"], [8, 512, D], F32, kind="ExternalInput").ap()
        edw = nc.dram_tensor(names["edw"], [8, D, 512], F32, kind="ExternalInput").ap()
        sgw = nc.dram_tensor(names["sgw"], [D, D], F32, kind="ExternalInput").ap()
        suw = nc.dram_tensor(names["suw"], [D, D], F32, kind="ExternalInput").ap()
        sdw = nc.dram_tensor(names["sdw"], [D, D], F32, kind="ExternalInput").ap()
        Sst = nc.dram_tensor(names["Sstack"], [8, 8], F32, kind="ExternalInput").ap()
        Cst = nc.dram_tensor(names["Cstack"], [8, 8], F32, kind="ExternalInput").ap()
        out = nc.dram_tensor(names["out"], [S, D], F32, kind="ExternalOutput").ap()
        gwT_h = nc.dram_tensor("gwT_scratch", [8, 128, 4096], BF16, kind="Internal")
        uwT_h = nc.dram_tensor("uwT_scratch", [8, 128, 4096], BF16, kind="Internal")
        dwT_h = nc.dram_tensor("dwT_scratch", [8, 128, 4096], BF16, kind="Internal")
        sgwT_h = nc.dram_tensor("sgwT_scratch", [128, 8192], BF16, kind="Internal")
        suwT_h = nc.dram_tensor("suwT_scratch", [128, 8192], BF16, kind="Internal")
        sdwT_h = nc.dram_tensor("sdwT_scratch", [128, 8192], BF16, kind="Internal")
    else:
        part = nc.dram_tensor(names["out"], [16, 1], F32, kind="ExternalOutput").ap()

    iota_np = np.broadcast_to(np.arange(8, dtype=np.float32), (128, 8)).copy()

    with ExitStack() as ctx:
        e = ctx.enter_context
        ident = e(nc.sbuf_tensor("ident", [128, 128], BF16))
        identf = e(nc.sbuf_tensor("identf", [128, 128], F32))
        onesf = e(nc.sbuf_tensor("onesf", [1, 128], F32))
        onescol = e(nc.sbuf_tensor("onescol", [128, 1], F32))
        iot = e(nc.sbuf_tensor("iot", [128, 8], F32))
        iop8 = e(nc.sbuf_tensor("iop8", [128, 8], F32))
        gws = e(nc.sbuf_tensor("gws", [128, 1024], F32))
        gwsum = e(nc.sbuf_tensor("gwsum", [128, 1], F32))
        gwTs = e(nc.sbuf_tensor("gwTs", [128, 64], F32))
        lbB = e(nc.sbuf_tensor("lbB", [128, 8], F32))
        gwsumB = e(nc.sbuf_tensor("gwsumB", [128, 8], F32))
        x1t = [e(nc.sbuf_tensor(f"x1t{i}", [128, 1024], F32)) for i in range(2)]
        x1Tt = e(nc.sbuf_tensor("x1Tt", [128, 1024], F32))
        stat = e(nc.sbuf_tensor("stat", [128, 12], F32))
        meanA = e(nc.sbuf_tensor("meanA", [128, NT], F32))
        rstdA = e(nc.sbuf_tensor("rstdA", [128, NT], F32))
        mvt = e(nc.sbuf_tensor("mvt", [128, 2], F32))
        rws = e(nc.sbuf_tensor("rws", [8, 128], F32))
        t8a = e(nc.sbuf_tensor("t8a", [128, 8], F32))
        t8b = e(nc.sbuf_tensor("t8b", [128, 8], F32))
        scoresb = e(nc.sbuf_tensor("scoresb", [128, 8], F32))
        w1 = e(nc.sbuf_tensor("w1", [128, 1], F32))
        w2 = e(nc.sbuf_tensor("w2", [128, 1], F32))
        i1 = e(nc.sbuf_tensor("i1", [128, 1], F32))
        i2 = e(nc.sbuf_tensor("i2", [128, 1], F32))
        m1 = e(nc.sbuf_tensor("m1", [128, 8], F32))
        m2 = e(nc.sbuf_tensor("m2", [128, 8], F32))
        s2t = e(nc.sbuf_tensor("s2t", [128, 8], F32))
        winv = e(nc.sbuf_tensor("winv", [128, 1], F32))
        tc = e(nc.sbuf_tensor("tc", [128, 16], F32))
        coefA = e(nc.sbuf_tensor("coefA", [128, NT * 8], F32))
        partsb = e(nc.sbuf_tensor("partsb", [16, 1], F32))

        block = e(nc.Block())
        b = B(nc, ctx, block)
        sl = b.ctr("sl"); sg = b.ctr("sg"); pe = b.ctr("pe")
        dv = b.ctr("dv"); sc = b.ctr("sc"); so = b.ctr("so"); sw = b.ctr("sw")

        identh = nc.inline_tensor(np.eye(128, dtype=BF), "identnp")
        identfh = nc.inline_tensor(np.eye(128, dtype=np.float32), "identfnp")
        ioth = nc.inline_tensor(iota_np, "iotanp")
        iop8h = nc.inline_tensor(iota_np + 8.0, "iop8np")

        def g0_vec0(v):
            v.memset(onesf[:, :], 1.0)
            v.memset(onescol[:, :], 1.0)
            dv.inc(v.memset(gws[:, :], 0.0))
        b.on("vector", g0_vec0)

        def g0_sync(s):
            sl.inc(s.dma_start(out=ident[:, :], in_=identh.ap()), 16)
            sl.inc(s.dma_start(out=identf[:, :], in_=identfh.ap()), 16)
            sl.inc(s.dma_start(out=iot[:, :], in_=ioth.ap()), 16)
            sl.inc(s.dma_start(out=iop8[:, :], in_=iop8h.ap()), 16)
            s.wait(dv)
            sl.inc(s.dma_start(out=gws[0:8, :], in_=gw), 16)
        b.on("sync", g0_sync)

        def g0_vec(v):
            v.wait(sl)
            dv.inc(v.reduce_sum(gwsum[0:8, :], gws[0:8, :], axis=AX.X))
        b.on("vector", g0_vec)

        with ExitStack() as g0ctx:
            pgt = g0ctx.enter_context(nc.psum_tensor("pgt", [128, 128], F32))
            pr8 = g0ctx.enter_context(nc.psum_tensor("pr8", [128, 8], F32))

            for it in range(8):
                def t_body(t, it=it):
                    t.wait(sl)
                    t.wait(dv)
                    if it > 0:
                        t.wait(dv)
                    pe.inc(t.transpose(pgt[:, :], gws[:, it * 128:(it + 1) * 128], identf[:, :]))
                b.on("tensor", t_body)

                def v_body(v, it=it):
                    v.wait(pe)
                    dv.inc(v.tensor_copy(gwTs[:, it * 8:(it + 1) * 8], pgt[:, 0:8]))
                b.on("vector", v_body)

            def t2_body(t):
                t.wait(dv)
                pe.inc(t.matmul(pr8[0:1, :], gwsum[0:8, 0:1], identf[0:8, 0:8], start=True, stop=True))
            b.on("tensor", t2_body)

            def v2_body(v):
                v.wait(pe)
                dv.inc(v.tensor_copy(t8a[0:1, :], pr8[0:1, :]))
            b.on("vector", v2_body)

            def s_body(s):
                sl.inc(s.dma_start(out=t8b[0:1, :], in_=lb), 16)
            b.on("sync", s_body)

            def t3_body(t):
                t.wait(dv)
                t.wait(sl)
                pe.inc(t.matmul(pr8[:, :], onesf[0:1, :], t8a[0:1, :], start=True, stop=True))
            b.on("tensor", t3_body)

            def v3_body(v):
                v.wait(pe)
                dv.inc(v.tensor_copy(gwsumB[:, :], pr8[:, :]))
            b.on("vector", v3_body)

            def t4_body(t):
                t.wait(dv)
                pe.inc(t.matmul(pr8[:, :], onesf[0:1, :], t8b[0:1, :], start=True, stop=True))
            b.on("tensor", t4_body)

            def v4_body(v):
                v.wait(pe)
                dv.inc(v.tensor_copy(lbB[:, :], pr8[:, :]))
            b.on("vector", v4_body)

            ppart = g0ctx.enter_context(nc.psum_tensor("ppart", [16, 1], F32))
            pxt = g0ctx.enter_context(nc.psum_tensor("pxt", [128, 1024], F32))
            for st in range(NT):
                xb = x1t[st % 2]

                def s_body(s, st=st, xb=xb):
                    s.wait(pe)
                    sl.inc(s.dma_start(out=xb[:, :], in_=x1[st * 128:(st + 1) * 128, :]), 16)
                b.on("sync", s_body)

                def v_body(v, st=st, xb=xb):
                    v.wait(sl)
                    v.wait(dv)
                    v.bn_stats(stat[:, 0:6], xb[:, 0:512])
                    dv.inc(v.bn_stats(stat[:, 6:12], xb[:, 512:1024]))
                    v.wait(dv)
                    dv.inc(v.bn_aggr(mvt[:, :], stat[:, :]))
                    v.wait(dv)
                    v.tensor_copy(meanA[:, st:st + 1], mvt[:, 0:1])
                    dv.inc(v.tensor_scalar_add(rstdA[:, st:st + 1], mvt[:, 1:2], EPS))
                b.on("vector", v_body)

                def sc_body(s, st=st):
                    s.wait(dv)
                    sc.inc(s.activation(rstdA[:, st:st + 1], rstdA[:, st:st + 1], AF.Sqrt))
                b.on("scalar", sc_body)

                def v2b(v, st=st):
                    v.wait(sc)
                    dv.inc(v.reciprocal(rstdA[:, st:st + 1], rstdA[:, st:st + 1]))
                b.on("vector", v2b)

                def t_body(t, st=st, xb=xb):
                    t.wait(sl)
                    t.wait(dv)
                    for it in range(8):
                        pe.inc(t.transpose(pxt[:, it * 128:(it + 1) * 128],
                                           xb[:, it * 128:(it + 1) * 128], identf[:, :]))
                b.on("tensor", t_body)

                def v3b(v, st=st):
                    v.wait(pe)
                    dv.inc(v.tensor_copy(x1Tt[:, :], pxt[:, :]))
                b.on("vector", v3b)

                def t2b(t, st=st):
                    t.wait(dv)
                    t.wait(sc)
                    for it in range(8):
                        i = t.matmul(pgt[0:8, 0:128], gwTs[:, it * 8:(it + 1) * 8],
                                     x1Tt[:, it * 128:(it + 1) * 128], start=(it == 0), stop=(it == 7))
                    pe.inc(i)
                b.on("tensor", t2b)

                def sc2b(s, st=st):
                    s.wait(pe)
                    sc.inc(s.copy(rws[:, :], pgt[0:8, 0:128]))
                b.on("scalar", sc2b)

                def t3b(t, st=st):
                    t.wait(sc)
                    t.wait(dv)
                    pe.inc(t.matmul(pr8[:, :], rws[:, :], identf[0:8, 0:8], start=True, stop=True))
                b.on("tensor", t3b)

                def v4b(v, st=st):
                    v.wait(pe)
                    dv.inc(v.tensor_scalar_mul(t8a[:, :], gwsumB[:, :], meanA[:, st:st + 1]))
                    v.wait(dv)
                    dv.inc(v.tensor_sub(t8b[:, :], pr8[:, :], t8a[:, :]))
                    v.wait(dv)
                    dv.inc(v.tensor_scalar_mul(t8b[:, :], t8b[:, :], rstdA[:, st:st + 1]))
                    v.wait(dv)
                    dv.inc(v.tensor_add(t8b[:, :], t8b[:, :], lbB[:, :]))
                b.on("vector", v4b)

                def sc3b(s, st=st):
                    s.wait(dv)
                    sc.inc(s.activation(scoresb[:, :], t8b[:, :], AF.Sigmoid))
                b.on("scalar", sc3b)

                def v5b(v, st=st):
                    v.wait(sc)
                    dv.inc(v.reduce_max(w1[:, :], scoresb[:, :], axis=AX.X))
                    v.wait(dv)
                    dv.inc(v.tensor_scalar(m1[:, :], scoresb[:, :], w1[:, 0:1], None, ALU.is_ge))
                    v.wait(dv)
                    dv.inc(v.scalar_tensor_tensor(t8a[:, :], m1[:, :], -8.0, iop8[:, :], ALU.mult, ALU.add))
                    v.wait(dv)
                    dv.inc(v.tensor_reduce(i1[:, :], t8a[:, :], axis=AX.X, op=ALU.min))
                    v.wait(dv)
                    dv.inc(v.tensor_scalar(m1[:, :], iot[:, :], i1[:, 0:1], None, ALU.is_equal))
                    v.wait(dv)
                    dv.inc(v.scalar_tensor_tensor(s2t[:, :], m1[:, :], -1e30, scoresb[:, :], ALU.mult, ALU.add))
                    v.wait(dv)
                    dv.inc(v.reduce_max(w2[:, :], s2t[:, :], axis=AX.X))
                    v.wait(dv)
                    dv.inc(v.tensor_scalar(m2[:, :], s2t[:, :], w2[:, 0:1], None, ALU.is_ge))
                    v.wait(dv)
                    dv.inc(v.scalar_tensor_tensor(t8a[:, :], m2[:, :], -8.0, iop8[:, :], ALU.mult, ALU.add))
                    v.wait(dv)
                    dv.inc(v.tensor_reduce(i2[:, :], t8a[:, :], axis=AX.X, op=ALU.min))
                    v.wait(dv)
                    dv.inc(v.tensor_scalar(m2[:, :], iot[:, :], i2[:, 0:1], None, ALU.is_equal))
                    v.wait(dv)
                    dv.inc(v.tensor_add(winv[:, :], w1[:, :], w2[:, :]))
                    v.wait(dv)
                    dv.inc(v.tensor_scalar_add(winv[:, :], winv[:, :], 1e-8))
                    v.wait(dv)
                    dv.inc(v.reciprocal(winv[:, :], winv[:, :]))
                    v.wait(dv)
                    dv.inc(v.tensor_scalar_mul(t8a[:, :], m1[:, :], w1[:, 0:1]))
                    v.wait(dv)
                    dv.inc(v.tensor_scalar_mul(t8b[:, :], m2[:, :], w2[:, 0:1]))
                    v.wait(dv)
                    dv.inc(v.tensor_add(t8a[:, :], t8a[:, :], t8b[:, :]))
                    v.wait(dv)
                    dv.inc(v.tensor_scalar_mul(tc[:, 0:8], t8a[:, :], winv[:, 0:1]))
                    v.wait(dv)
                    dv.inc(v.tensor_add(tc[:, 8:16], m1[:, :], m2[:, :]))
                    v.wait(dv)
                    dv.inc(v.tensor_copy(coefA[:, st * 8:(st + 1) * 8], tc[:, 8:16]))
                b.on("vector", v5b)

                def t4b(t, st=st):
                    t.wait(dv)
                    pe.inc(t.matmul(ppart[:, :], tc[:, :], onescol[:, :],
                                    start=(st == 0), stop=(st == NT - 1)))
                b.on("tensor", t4b)

            def v6b(v):
                v.wait(pe)
                dv.inc(v.tensor_copy(partsb[:, :], ppart[:, :]))
            b.on("vector", v6b)

            if mode == "partials":
                def s_out(s):
                    s.wait(dv)
                    so.inc(s.dma_start(out=part, in_=partsb[:, :]), 16)
                    s.wait(so)
                b.on("sync", s_out)
                return nc

            wmB = e(nc.sbuf_tensor("wmB", [128, 8], F32))
            Ssb = e(nc.sbuf_tensor("Ssb", [8, 8], F32))
            Csb = e(nc.sbuf_tensor("Csb", [8, 8], F32))
            Sv = e(nc.sbuf_tensor("Sv", [8, 1], F32))
            Cv = e(nc.sbuf_tensor("Cv", [8, 1], F32))

            def s2_body(s):
                sl.inc(s.dma_start(out=Ssb[:, :], in_=Sst), 16)
                sl.inc(s.dma_start(out=Csb[:, :], in_=Cst), 16)
            b.on("sync", s2_body)

            def v7b(v):
                v.wait(sl)
                v.wait(dv)
                dv.inc(v.reduce_sum(Sv[:, :], Ssb[:, :], axis=AX.X))
                dv.inc(v.reduce_sum(Cv[:, :], Csb[:, :], axis=AX.X))
                v.wait(dv)
                dv.inc(v.tensor_scalar_max(Cv[:, :], Cv[:, :], 1.0))
                v.wait(dv)
                dv.inc(v.reciprocal(Cv[:, :], Cv[:, :]))
                v.wait(dv)
                dv.inc(v.tensor_mul(Sv[:, :], Sv[:, :], Cv[:, :]))
            b.on("vector", v7b)

            def t5b(t):
                t.wait(dv)
                pe.inc(t.matmul(pr8[0:1, :], Sv[:, 0:1], identf[0:8, 0:8], start=True, stop=True))
            b.on("tensor", t5b)

            def v8b(v):
                v.wait(pe)
                dv.inc(v.tensor_copy(t8a[0:1, :], pr8[0:1, :]))
            b.on("vector", v8b)

            def t6b(t):
                t.wait(dv)
                pe.inc(t.matmul(pr8[:, :], onesf[0:1, :], t8a[0:1, :], start=True, stop=True))
            b.on("tensor", t6b)

            def v9b(v):
                v.wait(pe)
                dv.inc(v.tensor_copy(wmB[:, :], pr8[:, :]))
                v.wait(dv)
                for st in range(NT):
                    i = v.tensor_mul(coefA[:, st * 8:(st + 1) * 8], coefA[:, st * 8:(st + 1) * 8], wmB[:, :])
                dv.inc(i)
            b.on("vector", v9b)

        with ExitStack() as wctx:
            wstage = wctx.enter_context(nc.sbuf_tensor("wstage", [128, 8192], BF16))
            wTsb = wctx.enter_context(nc.sbuf_tensor("wTsb", [128, 8192], BF16))
            ptw = wctx.enter_context(nc.psum_tensor("ptw", [128, 1024], BF16))
            jobs = []
            for ee in range(8):
                jobs += [(egw[ee], gwT_h.ap()[ee], 4), (euw[ee], uwT_h.ap()[ee], 4),
                         (edw[ee], dwT_h.ap()[ee], 8)]
            jobs += [(sgw, sgwT_h.ap(), 8), (suw, suwT_h.ap(), 8), (sdw, sdwT_h.ap(), 8)]
            for ji, (src, dsth, nt) in enumerate(jobs):
                ncols = 1024 if nt == 8 else 1024
                ncols = src.shape[-1]

                def g_body(g, ji=ji, src=src, nt=nt, ncols=ncols):
                    g.wait(pe)
                    sg.inc(g.dma_start(
                        out=wstage[:, 0:nt * ncols].rearrange("p (t i) -> p t i", t=nt),
                        in_=src.rearrange("(t p) i -> p t i", p=128)), 16)
                b.on("gpsimd", g_body)

                nu = ncols // 128
                for u in range(nu):
                    def t_body(t, ji=ji, u=u, nt=nt, ncols=ncols):
                        t.wait(sg)
                        t.wait(sl)
                        t.wait(dv)
                        for tt in range(nt):
                            pe.inc(t.transpose(ptw[:, tt * 128:(tt + 1) * 128],
                                               wstage[:, tt * ncols + u * 128:tt * ncols + (u + 1) * 128],
                                               ident[:, :]))
                    b.on("tensor", t_body)

                    def v_body(v, ji=ji, u=u, nt=nt):
                        v.wait(pe)
                        v.wait(sw)
                        dv.inc(v.tensor_copy(wTsb[:, u * nt * 128:(u + 1) * nt * 128], ptw[:, 0:nt * 128]))
                    b.on("vector", v_body)

                def s_body(s, ji=ji, dsth=dsth, nt=nt, ncols=ncols):
                    s.wait(dv)
                    nu_ = ncols // 128
                    sw.inc(s.dma_start(out=dsth, in_=wTsb[:, 0:nu_ * nt * 128]), 16)
                b.on("sync", s_body)

        acc = e(nc.sbuf_tensor("acc", [128, 16 * 1024], F32))
        ln1T = e(nc.sbuf_tensor("ln1T", [128, 8 * 2048], BF16))
        lnt = e(nc.sbuf_tensor("lnt", [128, 1024], BF16))
        wgs = e(nc.sbuf_tensor("wgs", [128, 8192], BF16))
        wus = e(nc.sbuf_tensor("wus", [128, 8192], BF16))
        wds = e(nc.sbuf_tensor("wds", [128, 8192], BF16))
        silb = e(nc.sbuf_tensor("silb", [128, 512], BF16))
        actb = e(nc.sbuf_tensor("actb", [128, 8 * 512], BF16))

        for h in range(2):
            def s_body(s, h=h):
                s.wait(dv)
                s.wait(so)
                s.wait(sw)
                sl.inc(s.dma_start(out=acc[:, :].rearrange("p (t d) -> p t d", t=16),
                                   in_=x1[h * 2048:(h + 1) * 2048, :].rearrange("(t p) d -> p t d", p=128)), 16)
            b.on("sync", s_body)

            with nc.psum_tensor(f"plt_{h}", [128, 1024], BF16) as plt:
                for stt in range(16):
                    gst = h * 16 + stt

                    def v_body(v, h=h, stt=stt, gst=gst):
                        v.wait(sl)
                        v.wait(pe)
                        dv.inc(v.tensor_scalar(lnt[:, :], acc[:, stt * 1024:(stt + 1) * 1024],
                                               meanA[:, gst:gst + 1], rstdA[:, gst:gst + 1],
                                               ALU.subtract, ALU.mult))
                    b.on("vector", v_body)

                    def t_body(t, h=h, stt=stt):
                        t.wait(dv)
                        for it in range(8):
                            pe.inc(t.transpose(plt[:, it * 128:(it + 1) * 128],
                                               lnt[:, it * 128:(it + 1) * 128], ident[:, :]))
                    b.on("tensor", t_body)

                    def v2_body(v, h=h, stt=stt):
                        v.wait(pe)
                        for it in range(8):
                            i = v.tensor_copy(ln1T[:, it * 2048 + stt * 128:it * 2048 + (stt + 1) * 128],
                                              plt[:, it * 128:(it + 1) * 128])
                        dv.inc(i)
                    b.on("vector", v2_body)

            for ee in range(9):
                shared = (ee == 8)
                if shared:
                    def g_body(g, h=h):
                        g.wait(pe)
                        g.wait(sw)
                        sg.inc(g.dma_start(out=wgs[:, :], in_=sgwT_h.ap()), 16)
                        sg.inc(g.dma_start(out=wus[:, :], in_=suwT_h.ap()), 16)
                        sg.inc(g.dma_start(out=wds[:, :], in_=sdwT_h.ap()), 16)
                    b.on("gpsimd", g_body)
                else:
                    def g_body(g, h=h, ee=ee):
                        g.wait(pe)
                        g.wait(sw)
                        sg.inc(g.dma_start(out=wgs[:, 0:4096], in_=gwT_h.ap()[ee]), 16)
                        sg.inc(g.dma_start(out=wus[:, 0:4096], in_=uwT_h.ap()[ee]), 16)
                        sg.inc(g.dma_start(out=wds[:, 0:4096], in_=dwT_h.ap()[ee]), 16)
                    b.on("gpsimd", g_body)

                nI = 8 if shared else 4
                wgc, wuc, wdc = wgs, wus, wds
                for sc_ in range(4):
                    for Ih in range(nI // 4):
                        with ExitStack() as pctx:
                            pgu = [pctx.enter_context(nc.psum_tensor(f"pgu_{h}_{ee}_{sc_}_{Ih}_{i}", [128, 512], F32))
                                   for i in range(8)]

                            def t_body(t, h=h, ee=ee, sc_=sc_, Ih=Ih, wgc=wgc, wuc=wuc, nI=nI):
                                t.wait(sg)
                                t.wait(dv)
                                t.wait(sc)
                                for It4 in range(4):
                                    It = Ih * 4 + It4
                                    for it in range(8):
                                        t.matmul(pgu[It4][:, :],
                                                 wgc[:, it * (nI * 128) + It * 128:it * (nI * 128) + (It + 1) * 128],
                                                 ln1T[:, it * 2048 + sc_ * 512:it * 2048 + (sc_ + 1) * 512],
                                                 start=(it == 0), stop=(it == 7))
                                    for it in range(8):
                                        i = t.matmul(pgu[4 + It4][:, :],
                                                     wuc[:, it * (nI * 128) + It * 128:it * (nI * 128) + (It + 1) * 128],
                                                     ln1T[:, it * 2048 + sc_ * 512:it * 2048 + (sc_ + 1) * 512],
                                                     start=(it == 0), stop=(it == 7))
                                pe.inc(i)
                            b.on("tensor", t_body)

                            for It4 in range(4):
                                def sc_body(s, It4=It4):
                                    s.wait(pe)
                                    s.wait(dv)
                                    sc.inc(s.activation(silb[:, :], pgu[It4][:, :], AF.Silu))
                                b.on("scalar", sc_body)

                                def v_body(v, It4=It4, Ih=Ih):
                                    v.wait(sc)
                                    dv.inc(v.tensor_mul(actb[:, (Ih * 4 + It4) * 512:(Ih * 4 + It4 + 1) * 512],
                                                        silb[:, :], pgu[4 + It4][:, :]))
                                b.on("vector", v_body)

                    with ExitStack() as pctx:
                        pd = [pctx.enter_context(nc.psum_tensor(f"pd_{h}_{ee}_{sc_}_{i}", [128, 1024], F32))
                              for i in range(2)]
                        for stt4 in range(4):
                            gst = h * 16 + sc_ * 4 + stt4

                            def t_body(t, h=h, ee=ee, sc_=sc_, stt4=stt4, wdc=wdc, nI=nI):
                                t.wait(dv)
                                for It in range(nI):
                                    for n in range(2):
                                        i = t.matmul(pd[stt4 % 2][:, n * 512:(n + 1) * 512],
                                                     actb[:, It * 512 + stt4 * 128:It * 512 + (stt4 + 1) * 128],
                                                     wdc[:, It * 1024 + n * 512:It * 1024 + (n + 1) * 512],
                                                     start=(It == 0), stop=(It == nI - 1))
                                pe.inc(i)
                            b.on("tensor", t_body)

                            def v_body(v, h=h, ee=ee, sc_=sc_, stt4=stt4, gst=gst, shared=shared):
                                v.wait(pe)
                                stt = sc_ * 4 + stt4
                                coef = 1.0 if shared else coefA[:, gst * 8 + ee:gst * 8 + ee + 1]
                                dv.inc(v.scalar_tensor_tensor(acc[:, stt * 1024:(stt + 1) * 1024],
                                                              pd[stt4 % 2][:, :], coef,
                                                              acc[:, stt * 1024:(stt + 1) * 1024],
                                                              ALU.mult, ALU.add))
                            b.on("vector", v_body)

            def s2_body(s, h=h):
                s.wait(dv)
                so.inc(s.dma_start(out=out[h * 2048:(h + 1) * 2048, :].rearrange("(t p) d -> p t d", p=128),
                                   in_=acc[:, :].rearrange("p (t d) -> p t d", t=16)), 16)
            b.on("sync", s2_body)

        def fin(s):
            s.wait(so)
        b.on("sync", fin)
    return nc



def build_attn1(nc, names):
    x2 = nc.dram_tensor(names["x"], [S, D], F32, kind="ExternalInput").ap()
    wnames = ["p", "q", "k", "v", "o"]
    Wd = {w: nc.dram_tensor(names["W" + w], [D, D], F32, kind="ExternalInput").ap() for w in wnames}
    Bd = {w: nc.dram_tensor(names["b" + w], [1, D], F32, kind="ExternalInput").ap() for w in wnames}
    out = nc.dram_tensor(names["out"], [S, D], F32, kind="ExternalOutput").ap()
    ln2h = nc.dram_tensor("ln2T_scratch", [8, 128, S + 2], BF16, kind="Internal")

    bsum_np = np.zeros((128, 8 * 16), dtype=BF)
    bbc_np = np.zeros((16, 8 * 128), dtype=BF)
    for t in range(8):
        bsum_np[0:64, t * 16 + 2 * t] = 1
        bsum_np[64:128, t * 16 + 2 * t + 1] = 1
        bbc_np[2 * t, t * 128:t * 128 + 64] = 1
        bbc_np[2 * t + 1, t * 128 + 64:t * 128 + 128] = 1

    with ExitStack() as ctx:
        e = ctx.enter_context
        ident = e(nc.sbuf_tensor("ident", [128, 128], BF16))
        identf = e(nc.sbuf_tensor("identf", [128, 128], F32))
        bsum = e(nc.sbuf_tensor("bsum", [128, 8 * 16], BF16))
        bbc = e(nc.sbuf_tensor("bbc", [16, 8 * 128], BF16))
        WT = {w: e(nc.sbuf_tensor(f"WT{w}", [128, 8192], BF16)) for w in wnames}
        bcol = {w: e(nc.sbuf_tensor(f"bcol{w}", [128, 8], F32)) for w in wnames}
        x2t = [e(nc.sbuf_tensor(f"x2t{i}", [128, 1024], F32)) for i in range(2)]
        stat = e(nc.sbuf_tensor("stat", [128, 12], F32))
        mv = e(nc.sbuf_tensor("mv", [128, 2], F32))
        rstd = e(nc.sbuf_tensor("rstd", [128, 1], F32))
        lnt = e(nc.sbuf_tensor("lnt", [128, 1024], BF16))
        d1ctx = ExitStack()
        wstage = d1ctx.enter_context(nc.sbuf_tensor("wstage", [128, 8192], BF16))
        ln2T = d1ctx.enter_context(nc.sbuf_tensor("ln2T", [128, 8 * 4096], BF16))
        block = e(nc.Block())
        b = B(nc, ctx, block)
        sl = b.ctr("sl"); sg = b.ctr("sg"); pe = b.ctr("pe")
        dv = b.ctr("dv"); sc = b.ctr("sc"); sp = b.ctr("sp")
        sob = [b.ctr("sob0"), b.ctr("sob1")]

        identh = nc.inline_tensor(np.eye(128, dtype=BF), "identnp")
        identfh = nc.inline_tensor(np.eye(128, dtype=np.float32), "identfnp")
        bsumh = nc.inline_tensor(bsum_np, "bsumnp")
        bbch = nc.inline_tensor(bbc_np, "bbcnp")

        def d0_sync(s):
            sl.inc(s.dma_start(out=ident[:, :], in_=identh.ap()), 16)
            sl.inc(s.dma_start(out=identf[:, :], in_=identfh.ap()), 16)
            sl.inc(s.dma_start(out=bsum[:, :], in_=bsumh.ap()), 16)
            sl.inc(s.dma_start(out=bbc[:, :], in_=bbch.ap()), 16)
            for w in wnames:
                for m in range(8):
                    sl.inc(s.dma_start(out=bcol[w][:, m:m + 1],
                                       in_=Bd[w][0:1, m * 128:(m + 1) * 128].rearrange("o p -> p o")), 16)
        b.on("sync", d0_sync)

        def d0_vec(v):
            v.wait(sl)
            dv.inc(v.tensor_scalar_mul(bcol["q"][:, :], bcol["q"][:, :], 0.125))
        b.on("vector", d0_vec)

        for wi, w in enumerate(wnames):
            def g_body(g, w=w, wi=wi):
                if wi > 0:
                    g.wait(pe)
                sg.inc(g.dma_start(out=wstage[:, :].rearrange("p (t i) -> p t i", t=8),
                                   in_=Wd[w].rearrange("(t p) i -> p t i", p=128)), 16)
            b.on("gpsimd", g_body)

            with nc.psum_tensor(f"ptw_{w}", [128, 1024], BF16) as ptw:
                for u in range(8):
                    def t_body(t, w=w, u=u):
                        t.wait(sg)
                        t.wait(sl)
                        if u > 0:
                            t.wait(dv)
                        for tt in range(8):
                            pe.inc(t.transpose(ptw[:, tt * 128:(tt + 1) * 128],
                                               wstage[:, tt * 1024 + u * 128:tt * 1024 + (u + 1) * 128],
                                               ident[:, :]))
                    b.on("tensor", t_body)

                    def v_body(v, w=w, u=u):
                        v.wait(pe)
                        dv.inc(v.tensor_copy(WT[w][:, u * 1024:(u + 1) * 1024], ptw[:, :]))
                    b.on("vector", v_body)

        for st in range(NT):
            xb = x2t[st % 2]

            def s_body(s, st=st, xb=xb):
                s.wait(dv)
                sl.inc(s.dma_start(out=xb[:, :], in_=x2[st * 128:(st + 1) * 128, :]), 16)
            b.on("sync", s_body)

            def v_body(v, st=st, xb=xb):
                v.wait(sl)
                v.wait(dv)
                v.bn_stats(stat[:, 0:6], xb[:, 0:512])
                dv.inc(v.bn_stats(stat[:, 6:12], xb[:, 512:1024]))
                v.wait(dv)
                dv.inc(v.bn_aggr(mv[:, :], stat[:, :]))
                v.wait(dv)
                dv.inc(v.tensor_scalar_add(rstd[:, 0:1], mv[:, 1:2], EPS))
            b.on("vector", v_body)

            def s2_body(s):
                s.wait(dv)
                sc.inc(s.activation(rstd[:, 0:1], rstd[:, 0:1], AF.Sqrt))
            b.on("scalar", s2_body)

            def v2_body(v, st=st, xb=xb):
                v.wait(sc)
                dv.inc(v.reciprocal(rstd[:, 0:1], rstd[:, 0:1]))
                v.wait(dv)
                dv.inc(v.tensor_scalar(lnt[:, :], xb[:, :], mv[:, 0:1], rstd[:, 0:1],
                                       ALU.subtract, ALU.mult))
            b.on("vector", v2_body)

            with nc.psum_tensor(f"plt_{st}", [128, 1024], BF16) as plt:
                def t_body(t, st=st):
                    t.wait(dv)
                    for it in range(8):
                        pe.inc(t.transpose(plt[:, it * 128:(it + 1) * 128],
                                           lnt[:, it * 128:(it + 1) * 128], ident[:, :]))
                b.on("tensor", t_body)

                def v3_body(v, st=st):
                    v.wait(pe)
                    for it in range(8):
                        i = v.tensor_copy(ln2T[:, it * 4096 + st * 128:it * 4096 + (st + 1) * 128],
                                          plt[:, it * 128:(it + 1) * 128])
                    dv.inc(i)
                b.on("vector", v3_body)

        def spill(s):
            s.wait(dv)
            for it in range(8):
                sp.inc(s.dma_start(out=ln2h.ap()[it, :, 1:S + 1], in_=ln2T[:, it * 4096:(it + 1) * 4096]), 16)
                sp.inc(s.dma_start(out=ln2h.ap()[it, :, 0:1], in_=ln2T[:, it * 4096 + 4095:it * 4096 + 4096],
                                   allow_slow_non_contiguous=True), 16)
                sp.inc(s.dma_start(out=ln2h.ap()[it, :, S + 1:S + 2], in_=ln2T[:, it * 4096:it * 4096 + 1],
                                   allow_slow_non_contiguous=True), 16)
        b.on("sync", spill)
        d1ctx.close()

        lnc = e(nc.sbuf_tensor("lnc", [128, 8 * 514], BF16))
        xpc = e(nc.sbuf_tensor("xpc", [128, 8 * 514], BF16))
        qc = e(nc.sbuf_tensor("qc", [128, 8 * 512], BF16))
        kc = e(nc.sbuf_tensor("kc", [128, 8 * 514], BF16))
        vc = e(nc.sbuf_tensor("vc", [128, 8 * 514], BF16))
        prodb = [e(nc.sbuf_tensor(f"prodb{i}", [128, 512], BF16)) for i in range(2)]
        expb = e(nc.sbuf_tensor("expb", [16, 3 * 512], F32))
        dnm = e(nc.sbuf_tensor("dnm", [16, 512], F32))
        attnb = e(nc.sbuf_tensor("attnb", [16, 3 * 512], BF16))
        acc = e(nc.sbuf_tensor("acc", [128, 8 * 512], F32))
        tmpb = e(nc.sbuf_tensor("tmpb", [128, 512], F32))
        attnoT = e(nc.sbuf_tensor("attnoT", [128, 8 * 512], BF16))
        yT = e(nc.sbuf_tensor("yT", [128, 8 * 512], F32))
        outt = [e(nc.sbuf_tensor(f"outt{i}", [128, 1024], F32)) for i in range(2)]

        def mm_proj(t, w, m, dst_main, dst_halo, src, width):
            for it in range(8):
                i = t.matmul(dst_main[:, :], WT[w][:, it * 1024 + m * 128:it * 1024 + (m + 1) * 128],
                             src[:, it * 514 + 1:it * 514 + 513],
                             start=(it == 0), stop=(it == 7))
            if width == 514:
                for it in range(8):
                    i = t.matmul(dst_halo[:, 0:2], WT[w][:, it * 1024 + m * 128:it * 1024 + (m + 1) * 128],
                                 src[:, it * 514:it * 514 + 514:513],
                                 start=(it == 0), stop=(it == 7))
            return i

        for c in range(8):
            def s_body(s, c=c):
                s.wait(pe)
                if c == 0:
                    s.wait(sp)
                sl.inc(s.dma_start(out=lnc[:, :].rearrange("p (it w) -> p it w", it=8),
                                   in_=ln2h.ap()[:, :, c * 512:c * 512 + 514].rearrange("it p w -> p it w")), 16)
            b.on("sync", s_body)

            with ExitStack() as sctx:
                pmain = [sctx.enter_context(nc.psum_tensor(f"pm_{c}_{i}", [128, 512], F32)) for i in range(2)]
                phalo = [sctx.enter_context(nc.psum_tensor(f"phh_{c}_{i}", [128, 2], F32)) for i in range(2)]
                for m in range(8):
                    def t_body(t, c=c, m=m):
                        t.wait(sl)
                        t.wait(dv)
                        t.wait(sc)
                        pe.inc(mm_proj(t, "p", m, pmain[m % 2], phalo[m % 2], lnc, 514))
                    b.on("tensor", t_body)

                    def sc_body(s, c=c, m=m):
                        s.wait(pe)
                        s.wait(dv)
                        s.activation(xpc[:, m * 514 + 1:m * 514 + 513], pmain[m % 2][:, :], AF.Identity,
                                     bias=bcol["p"][:, m:m + 1])
                        i = s.activation(xpc[:, m * 514:m * 514 + 514:513], phalo[m % 2][:, :], AF.Identity,
                                         bias=bcol["p"][:, m:m + 1])
                        sc.inc(i)
                    b.on("scalar", sc_body)

                for m in range(8):
                    def t_body(t, c=c, m=m):
                        t.wait(sc)
                        pe.inc(mm_proj(t, "q", m, pmain[m % 2], None, xpc, 512))
                    b.on("tensor", t_body)

                    def sc_body(s, c=c, m=m):
                        s.wait(pe)
                        sc.inc(s.activation(qc[:, m * 512:(m + 1) * 512], pmain[m % 2][:, :], AF.Identity,
                                            bias=bcol["q"][:, m:m + 1], scale=0.125))
                    b.on("scalar", sc_body)

                for dst, w in ((kc, "k"), (vc, "v")):
                    for m in range(8):
                        def t_body(t, c=c, m=m, w=w):
                            t.wait(sc)
                            pe.inc(mm_proj(t, w, m, pmain[m % 2], phalo[m % 2], xpc, 514))
                        b.on("tensor", t_body)

                        def sc_body(s, c=c, m=m, w=w, dst=dst):
                            s.wait(pe)
                            s.activation(dst[:, m * 514 + 1:m * 514 + 513], pmain[m % 2][:, :], AF.Identity,
                                         bias=bcol[w][:, m:m + 1])
                            sc.inc(s.activation(dst[:, m * 514:m * 514 + 514:513], phalo[m % 2][:, :],
                                                AF.Identity, bias=bcol[w][:, m:m + 1]))
                        b.on("scalar", sc_body)

            with ExitStack() as sctx:
                psc = sctx.enter_context(nc.psum_tensor(f"psc_{c}", [16, 3 * 512], F32))
                for n in range(3):
                    for t8 in range(8):
                        pb_ = prodb[(n * 8 + t8) % 2]

                        def v_body(v, c=c, n=n, t8=t8, pb_=pb_):
                            v.wait(sc)
                            v.wait(pe)
                            dv.inc(v.tensor_mul(pb_[:, :], qc[:, t8 * 512:(t8 + 1) * 512],
                                                kc[:, t8 * 514 + n:t8 * 514 + n + 512]))
                        b.on("vector", v_body)

                        def t_body(t, c=c, n=n, t8=t8, pb_=pb_):
                            t.wait(dv)
                            pe.inc(t.matmul(psc[:, n * 512:(n + 1) * 512], bsum[:, t8 * 16:(t8 + 1) * 16],
                                            pb_[:, :], start=(t8 == 0), stop=(t8 == 7)))
                        b.on("tensor", t_body)

                def sc_body(s, c=c):
                    s.wait(pe)
                    sc.inc(s.activation(expb[:, :], psc[:, :], AF.Exp))
                b.on("scalar", sc_body)

                def v_body(v, c=c):
                    v.wait(sc)
                    dv.inc(v.tensor_add(dnm[:, :], expb[:, 0:512], expb[:, 512:1024]))
                    v.wait(dv)
                    dv.inc(v.tensor_add(dnm[:, :], dnm[:, :], expb[:, 1024:1536]))
                    v.wait(dv)
                    dv.inc(v.reciprocal(dnm[:, :], dnm[:, :]))
                    v.wait(dv)
                    for n in range(3):
                        i = v.tensor_mul(attnb[:, n * 512:(n + 1) * 512],
                                         expb[:, n * 512:(n + 1) * 512], dnm[:, :])
                    dv.inc(i)
                b.on("vector", v_body)

            with ExitStack() as sctx:
                pb2 = [sctx.enter_context(nc.psum_tensor(f"pb2_{c}_{i}", [128, 512], F32)) for i in range(2)]
                for n in range(3):
                    for t8 in range(8):
                        pp = pb2[(n * 8 + t8) % 2]

                        def t_body(t, c=c, n=n, t8=t8, pp=pp):
                            t.wait(dv)
                            pe.inc(t.matmul(pp[:, :], bbc[:, t8 * 128:(t8 + 1) * 128],
                                            attnb[:, n * 512:(n + 1) * 512], start=True, stop=True))
                        b.on("tensor", t_body)

                        def v_body(v, c=c, n=n, t8=t8, pp=pp):
                            v.wait(pe)
                            vsl = vc[:, t8 * 514 + n:t8 * 514 + n + 512]
                            if n == 0:
                                dv.inc(v.tensor_mul(acc[:, t8 * 512:(t8 + 1) * 512], pp[:, :], vsl))
                            else:
                                dv.inc(v.tensor_mul(tmpb[:, :], pp[:, :], vsl))
                                v.wait(dv)
                                dst = acc if n == 1 else attnoT
                                dv.inc(v.tensor_add(dst[:, t8 * 512:(t8 + 1) * 512],
                                                    acc[:, t8 * 512:(t8 + 1) * 512], tmpb[:, :]))
                        b.on("vector", v_body)

            with ExitStack() as sctx:
                pyo = [sctx.enter_context(nc.psum_tensor(f"pyo_{c}_{i}", [128, 512], F32)) for i in range(2)]
                for m in range(8):
                    def t_body(t, c=c, m=m):
                        t.wait(dv)
                        t.wait(sc)
                        for it in range(8):
                            pe.inc(t.matmul(pyo[m % 2][:, :], WT["o"][:, it * 1024 + m * 128:it * 1024 + (m + 1) * 128],
                                            attnoT[:, it * 512:(it + 1) * 512],
                                            start=(it == 0), stop=(it == 7)))
                    b.on("tensor", t_body)

                    def sc_body(s, c=c, m=m):
                        s.wait(pe)
                        sc.inc(s.activation(yT[:, m * 512:(m + 1) * 512], pyo[m % 2][:, :], AF.Identity,
                                            bias=bcol["o"][:, m:m + 1]))
                    b.on("scalar", sc_body)

            with ExitStack() as sctx:
                pxt = [sctx.enter_context(nc.psum_tensor(f"pxt_{c}_{i}", [128, 1024], F32)) for i in range(2)]
                for st in range(4):
                    gt = c * 4 + st
                    xb = x2t[st % 2]
                    ob = outt[st % 2]

                    def s_body(s, c=c, st=st, xb=xb):
                        s.wait(dv)
                        sl.inc(s.dma_start(out=xb[:, :], in_=x2[(c * 4 + st) * 128:(c * 4 + st + 1) * 128, :]), 16)
                    b.on("sync", s_body)

                    def t_body(t, c=c, st=st):
                        t.wait(sc)
                        t.wait(dv)
                        for m in range(8):
                            pe.inc(t.transpose(pxt[st % 2][:, m * 128:(m + 1) * 128],
                                               yT[:, m * 512 + st * 128:m * 512 + (st + 1) * 128],
                                               identf[:, :]))
                    b.on("tensor", t_body)

                    def v_body(v, c=c, st=st, xb=xb, ob=ob, sob_=sob[st % 2]):
                        v.wait(pe)
                        v.wait(sl)
                        v.wait(sob_)
                        dv.inc(v.tensor_add(ob[:, :], pxt[st % 2][:, :], xb[:, :]))
                    b.on("vector", v_body)

                    def s2_body(s, c=c, st=st, ob=ob, sob_=sob[st % 2], gt=gt):
                        s.wait(dv)
                        sob_.inc(s.dma_start(out=out[gt * 128:(gt + 1) * 128, :], in_=ob[:, :]), 16)
                    b.on("sync", s2_body)

        def fin(s):
            s.wait(sob[0])
            s.wait(sob[1])
        b.on("sync", fin)
    return nc



NCORES = 8


def kernel(**inputs):
    ii = {k: np.ascontiguousarray(v) for k, v in inputs.items()}
    x = ii["L0_attn_in_features"]
    cores = list(range(NCORES))

    ncA = bass.Bass("TRN2", target_bir_lowering=False, debug=False)
    build_attn0(ncA, dict(x="x", wq="wq", wk="wk", wv="wv", wo="wo", out="out"))
    mapsA = [dict(x=x[i], wq=ii["L0_attn_q_proj_weight"], wk=ii["L0_attn_k_proj_weight"],
                  wv=ii["L0_attn_v_proj_weight"], wo=ii["L0_attn_o_proj_weight"])
             for i in cores]
    resA = run_bass_kernel_spmd(ncA, mapsA, core_ids=cores)
    x1 = [resA.results[i]["out"] for i in cores]

    moe_names = dict(x="x1", gate_w="gate_w", lb_bias="lb_bias", egw="egw", euw="euw",
                     edw="edw", sgw="sgw", suw="suw", sdw="sdw",
                     Sstack="Sstack", Cstack="Cstack", out="out")
    lbr = ii["L0_ffn_lb_bias"].reshape(1, 8)
    ncB = bass.Bass("TRN2", target_bir_lowering=False, debug=False)
    build_moe(ncB, moe_names, "partials")
    mapsB = [dict(x1=x1[i], gate_w=ii["L0_ffn_gate_w"], lb_bias=lbr) for i in cores]
    resB = run_bass_kernel_spmd(ncB, mapsB, core_ids=cores)
    Sg = np.zeros((8, 8), np.float32)
    Cg = np.zeros((8, 8), np.float32)
    for i in cores:
        p = resB.results[i]["out"][:, 0]
        Sg[:, i] = p[0:8]
        Cg[:, i] = p[8:16]

    ncC = bass.Bass("TRN2", target_bir_lowering=False, debug=False)
    build_moe(ncC, moe_names, "full")
    mapsC = [dict(x1=x1[i], gate_w=ii["L0_ffn_gate_w"], lb_bias=lbr,
                  egw=ii["L0_ffn_expert_gate_w"], euw=ii["L0_ffn_expert_up_w"],
                  edw=ii["L0_ffn_expert_down_w"], sgw=ii["L0_ffn_shared_gate_w"],
                  suw=ii["L0_ffn_shared_up_w"], sdw=ii["L0_ffn_shared_down_w"],
                  Sstack=Sg, Cstack=Cg) for i in cores]
    resC = run_bass_kernel_spmd(ncC, mapsC, core_ids=cores)
    x2 = [resC.results[i]["out"] for i in cores]

    namesD = dict(x="x2", out="out")
    for w in "pqkvo":
        namesD["W" + w] = "W" + w
        namesD["b" + w] = "b" + w
    ncD = bass.Bass("TRN2", target_bir_lowering=False, debug=False)
    build_attn1(ncD, namesD)
    mapsD = []
    for i in cores:
        m = dict(x2=x2[i])
        for w in "pqkvo":
            m["W" + w] = ii["L1_attn_W" + w]
            m["b" + w] = ii["L1_attn_b" + w].reshape(1, D)
        mapsD.append(m)
    resD = run_bass_kernel_spmd(ncD, mapsD, core_ids=cores)
    x3 = [resD.results[i]["out"] for i in cores]

    ncE = bass.Bass("TRN2", target_bir_lowering=False, debug=False)
    build_ffn1(ncE, dict(x="x3", k0="k0", k1="k1", k2="k2", s0="s0", s1="s1", s2="s2",
                         out="out"))
    s0r = np.asarray(ii["L1_ffn_s0"], np.float32).reshape(1, 1)
    s1r = np.asarray(ii["L1_ffn_s1"], np.float32).reshape(1, 1)
    s2r = np.asarray(ii["L1_ffn_s2"], np.float32).reshape(1, 1)
    mapsE = [dict(x3=x3[i], k0=ii["L1_ffn_k0"].view(np.uint8), k1=ii["L1_ffn_k1"].view(np.uint8),
                  k2=ii["L1_ffn_k2"].view(np.uint8), s0=s0r, s1=s1r, s2=s2r) for i in cores]
    resE = run_bass_kernel_spmd(ncE, mapsE, core_ids=cores)

    return np.stack([resE.results[i]["out"] for i in cores]).astype(np.float32)


# version 02
# speedup vs baseline: 1.0075x; 1.0075x over previous
import numpy as np
import ml_dtypes
from contextlib import ExitStack

import concourse.bass as bass
import concourse.mybir as mybir
from concourse.bass_utils import run_bass_kernel_spmd

F32 = mybir.dt.float32
BF16 = mybir.dt.bfloat16
U8 = mybir.dt.uint8
AF = mybir.ActivationFunctionType
ALU = mybir.AluOpType
AX = mybir.AxisListType

S = 4096
D = 1024
NT = S // 128
EPS = 1e-5

BF = ml_dtypes.bfloat16



class Ctr:

    def __init__(self, nc, ctx, name):
        self.sem = ctx.enter_context(nc.semaphore(name))
        self.val = 0

    def inc(self, inst, n=1):
        inst.then_inc(self.sem, n)
        self.val += n
        return self.val


class B:

    def __init__(self, nc, ctx, block):
        self.nc = nc
        self.ctx = ctx
        self.block = block
        self.c = {}
        self._last_wait = {}

    def ctr(self, name):
        if name not in self.c:
            self.c[name] = Ctr(self.nc, self.ctx, name)
        return self.c[name]

    def on(self, engine, fn):
        getattr(self.block, engine)(lambda eng: fn(W(self, engine, eng)))

    def barrier(self):
        cs = list(self.c.values())
        for eng in ("sync", "gpsimd", "tensor", "vector", "scalar"):
            def f(w, cs=cs):
                for c in cs:
                    w.wait(c)
            self.on(eng, f)


class W:

    def __init__(self, b, name, eng):
        self.b = b
        self.name = name
        self.eng = eng

    def wait(self, ctr, val=None):
        v = ctr.val if val is None else val
        if v <= 0:
            return
        key = (self.name, id(ctr))
        if self.b._last_wait.get(key, -1) >= v:
            return
        self.eng.wait_ge(ctr.sem, v)
        self.b._last_wait[key] = v

    def __getattr__(self, item):
        return getattr(self.eng, item)



def _dram(nc, names, key, shape, dt, kind):
    v = names[key]
    if isinstance(v, str):
        return nc.dram_tensor(v, shape, dt, kind=kind).ap()
    return v


def build_ffn1(nc, names, env=None, px='f1_'):
    x3 = _dram(nc, names, "x", [S, D], F32, "ExternalInput")
    k0 = _dram(nc, names, "k0", [2 * D, 4096], U8, "ExternalInput")
    k1 = _dram(nc, names, "k1", [4096, 4096], U8, "ExternalInput")
    k2 = _dram(nc, names, "k2", [4096, 1024], U8, "ExternalInput")
    s0 = _dram(nc, names, "s0", [1, 1], F32, "ExternalInput")
    s1 = _dram(nc, names, "s1", [1, 1], F32, "ExternalInput")
    s2 = _dram(nc, names, "s2", [1, 1], F32, "ExternalInput")
    out = _dram(nc, names, "out", [S, D], F32, "ExternalOutput")
    w0p = nc.dram_tensor("w0p_scratch", [8, 128, 4096], BF16, kind="Internal")
    dbg = None
    if names.get("dbg"):
        dbg = {k: nc.dram_tensor("dbg_" + k, shp, dt, kind="ExternalOutput").ap()
               for k, shp, dt in [("ln3T", [128, 8 * 512], BF16), ("r1", [128, 32 * 512], BF16),
                                  ("r2", [128, 32 * 512], BF16), ("cs1", [1, 512], F32),
                                  ("cs2", [1, 512], F32), ("w0p", [8, 128, 4096], BF16)]}

    ident_np = np.eye(128, dtype=BF)

    with ExitStack() as ctx:
        e = ctx.enter_context
        ident = e(nc.sbuf_tensor(f"{px}ident", [128, 128], BF16))
        onesf = e(nc.sbuf_tensor(f"{px}onesf", [1, 128], F32))
        onecol_bf = e(nc.sbuf_tensor(f"{px}onecol_bf", [128, 1], BF16))
        nhcol_bf = e(nc.sbuf_tensor(f"{px}nhcol_bf", [1, 128], F32))
        nhrow_bf = e(nc.sbuf_tensor(f"{px}nhrow_bf", [1, 1024], F32))
        svals = e(nc.sbuf_tensor(f"{px}svals", [1, 3], F32))
        vec2s0 = e(nc.sbuf_tensor(f"{px}vec2s0", [128, 1], F32))
        vec4s1 = e(nc.sbuf_tensor(f"{px}vec4s1", [128, 1], F32))
        vec4s2 = e(nc.sbuf_tensor(f"{px}vec4s2", [128, 1], F32))
        w0a = e(nc.sbuf_tensor(f"{px}w0a", [128, 4096], BF16))
        w0b = e(nc.sbuf_tensor(f"{px}w0b", [128, 4096], BF16))
        w0d = e(nc.sbuf_tensor(f"{px}w0d", [128, 4096], BF16))
        x3c = e(nc.sbuf_tensor(f"{px}x3c", [128, 4 * 1024], F32))
        stat = e(nc.sbuf_tensor(f"{px}stat", [128, 12], F32))
        mv = e(nc.sbuf_tensor(f"{px}mv", [128, 2], F32))
        rstd = e(nc.sbuf_tensor(f"{px}rstd", [128, 1], F32))
        lnt = e(nc.sbuf_tensor(f"{px}lnt", [128, 1024], BF16))
        ln3T = e(nc.sbuf_tensor(f"{px}ln3T", [128, 8 * 512], BF16))
        r1 = e(nc.sbuf_tensor(f"{px}r1", [128, 32 * 512], BF16))
        r2 = e(nc.sbuf_tensor(f"{px}r2", [128, 32 * 512], BF16))
        wblk = [e(nc.sbuf_tensor(f"{px}wblk{i}", [128, 1024], BF16)) for i in range(2)]
        csum1 = e(nc.sbuf_tensor(f"{px}csum1", [1, 512], F32))
        csum2 = e(nc.sbuf_tensor(f"{px}csum2", [1, 512], F32))
        outb = [e(nc.sbuf_tensor(f"{px}outb{i}", [128, 1024], F32)) for i in range(2)]

        if env is None:
            block = e(nc.Block())
            b = B(nc, ctx, block)
        else:
            block = env["block"]
            b = env["b"]
        sl = b.ctr("sl")
        sg = b.ctr("sg")
        so = b.ctr("so")
        pe = b.ctr("pe")
        dv = b.ctr("dv")
        sc = b.ctr("sc")

        identh = nc.inline_tensor(ident_np, f"{px}identnp")

        def p0_sync(s):
            sl.inc(s.dma_start(out=ident[:, :], in_=identh.ap()), 16)
            sl.inc(s.dma_start(out=svals[0:1, 0:1], in_=s0), 16)
            sl.inc(s.dma_start(out=svals[0:1, 1:2], in_=s1), 16)
            sl.inc(s.dma_start(out=svals[0:1, 2:3], in_=s2), 16)
        b.on("sync", p0_sync)

        def p0_vec(v):
            v.memset(onesf[:, :], 1.0)
            v.memset(onecol_bf[:, :], 1.0)
            v.memset(nhcol_bf[:, :], -0.5)
            dv.inc(v.memset(nhrow_bf[:, :], -0.5))
        b.on("vector", p0_vec)

        with nc.psum_tensor(f"{px}pb0", [128, 512], F32) as pb0:
            for i, (src, dst, m) in enumerate([(0, vec2s0, 2.0), (1, vec4s1, 4.0), (2, vec4s2, 4.0)]):
                def t_body(t, i=i, src=src):
                    t.wait(sl)
                    t.wait(dv)
                    pe.inc(t.matmul(pb0[:, 0:1], onesf[0:1, :], svals[0:1, src:src + 1],
                                    start=True, stop=True))
                b.on("tensor", t_body)

                def v_body(v, dst=dst, m=m):
                    v.wait(pe)
                    dv.inc(v.tensor_scalar_mul(dst[:, 0:1], pb0[:, 0:1], float(m)))
                b.on("vector", v_body)

        for it in range(8):
            def g_body(g, it=it):
                g.wait(so)
                sg.inc(g.dma_start(out=w0a[:, :], in_=k0[it * 128:(it + 1) * 128, :]), 16)
                sg.inc(g.dma_start(out=w0b[:, :], in_=k0[1024 + it * 128:1024 + (it + 1) * 128, :]), 16)
            b.on("gpsimd", g_body)

            def v_body(v):
                v.wait(sg)
                v.wait(so)
                dv.inc(v.tensor_sub(w0d[:, :], w0a[:, :], w0b[:, :]))
            b.on("vector", v_body)

            def s_body(s, it=it):
                s.wait(dv)
                so.inc(s.dma_start(out=w0p.ap()[it, :, :], in_=w0d[:, :]), 16)
            b.on("sync", s_body)

        for c in range(8):
            for st in range(4):
                gt = c * 4 + st

                def s_body(s, st=st, gt=gt):
                    s.wait(dv)
                    sl.inc(s.dma_start(out=x3c[:, st * 1024:(st + 1) * 1024],
                                       in_=x3[gt * 128:(gt + 1) * 128, :]), 16)
                b.on("sync", s_body)

                def v_body(v, st=st):
                    v.wait(sl)
                    v.wait(dv)
                    v.bn_stats(stat[:, 0:6], x3c[:, st * 1024:st * 1024 + 512])
                    dv.inc(v.bn_stats(stat[:, 6:12], x3c[:, st * 1024 + 512:st * 1024 + 1024]))
                    v.wait(dv)
                    dv.inc(v.bn_aggr(mv[:, :], stat[:, :]))
                    v.wait(dv)
                    dv.inc(v.tensor_scalar_add(rstd[:, 0:1], mv[:, 1:2], EPS))
                b.on("vector", v_body)

                def s2_body(s):
                    s.wait(dv)
                    sc.inc(s.activation(rstd[:, 0:1], rstd[:, 0:1], AF.Sqrt))
                b.on("scalar", s2_body)

                def v2_body(v, st=st):
                    v.wait(sc)
                    dv.inc(v.reciprocal(rstd[:, 0:1], rstd[:, 0:1]))
                    v.wait(dv)
                    dv.inc(v.tensor_scalar(lnt[:, :], x3c[:, st * 1024:(st + 1) * 1024],
                                           mv[:, 0:1], rstd[:, 0:1], ALU.subtract, ALU.mult))
                b.on("vector", v2_body)

                with nc.psum_tensor(f"{px}ptr_{c}_{st}", [128, 1024], BF16) as ptr:
                    def t_body(t, st=st):
                        t.wait(dv)
                        t.wait(sl)
                        for it in range(8):
                            i = pe.inc(t.transpose(ptr[:, it * 128:(it + 1) * 128],
                                                   lnt[:, it * 128:(it + 1) * 128], ident[:, :]))
                    b.on("tensor", t_body)

                    def v3_body(v, st=st):
                        v.wait(pe)
                        for it in range(8):
                            i = v.tensor_copy(ln3T[:, it * 512 + st * 128:it * 512 + (st + 1) * 128],
                                              ptr[:, it * 128:(it + 1) * 128])
                        dv.inc(i)
                    b.on("vector", v3_body)

            with ExitStack() as sctx:
                pg = [sctx.enter_context(nc.psum_tensor(f"{px}pg_{c}_{j}", [128, 512], F32))
                      for j in range(8)]
                for mg in range(4):
                    for it in range(8):
                        buf = wblk[it % 2]

                        def g_body(g, mg=mg, it=it, buf=buf):
                            if mg + it > 0:
                                g.wait(pe)
                            sg.inc(g.dma_start(out=buf[:, :],
                                               in_=w0p.ap()[it, :, mg * 1024:(mg + 1) * 1024]), 16)
                        b.on("gpsimd", g_body)

                        def t_body(t, mg=mg, it=it, buf=buf):
                            t.wait(sg)
                            if it == 0:
                                t.wait(sc)
                            if mg == 0 and it == 0:
                                t.wait(dv)
                            for j in range(8):
                                pe.inc(t.matmul(pg[j][:, :], buf[:, j * 128:(j + 1) * 128],
                                                ln3T[:, it * 512:(it + 1) * 512],
                                                start=(it == 0), stop=(it == 7)))
                        b.on("tensor", t_body)

                    def s_body(s, mg=mg):
                        s.wait(pe)
                        s.wait(dv)
                        for j in range(8):
                            sc.inc(s.activation(r1[:, (mg * 8 + j) * 512:(mg * 8 + j + 1) * 512],
                                                pg[j][:, :], AF.Relu, scale=vec2s0[:, 0:1]))
                    b.on("scalar", s_body)

            if dbg is not None and c == 0:
                def dmp1(s):
                    s.wait(sc)
                    s.wait(dv)
                    so.inc(s.dma_start(out=dbg["ln3T"], in_=ln3T[:, :]), 16)
                    so.inc(s.dma_start(out=dbg["r1"], in_=r1[:, :]), 16)
                b.on("sync", dmp1)

            with ExitStack() as sctx:
                with nc.psum_tensor(f"{px}pcs_{c}", [1, 512], F32) as pcs:
                    def t_body(t, pcs=pcs):
                        t.wait(sc)
                        for kb in range(32):
                            pe.inc(t.matmul(pcs[:, :], onecol_bf[:, 0:1], r1[:, kb * 512:(kb + 1) * 512],
                                            start=(kb == 0), stop=(kb == 31)))
                    b.on("tensor", t_body)

                    def v_body(v, pcs=pcs):
                        v.wait(pe)
                        dv.inc(v.tensor_copy(csum1[0:1, :], pcs[0:1, :]))
                    b.on("vector", v_body)

                pg = [sctx.enter_context(nc.psum_tensor(f"{px}ph_{c}_{j}", [128, 512], F32))
                      for j in range(8)]
                for mg in range(4):
                    for kb in range(32):
                        buf = wblk[kb % 2]

                        def g_body(g, mg=mg, kb=kb, buf=buf):
                            g.wait(pe)
                            sg.inc(g.dma_start(out=buf[:, :],
                                               in_=k1[kb * 128:(kb + 1) * 128, mg * 1024:(mg + 1) * 1024]), 16)
                        b.on("gpsimd", g_body)

                        def t_body(t, mg=mg, kb=kb, buf=buf):
                            t.wait(sg)
                            if kb == 0:
                                t.wait(sc)
                                t.wait(dv)
                            for j in range(8):
                                pe.inc(t.matmul(pg[j][:, :], buf[:, j * 128:(j + 1) * 128],
                                                r1[:, kb * 512:(kb + 1) * 512],
                                                start=(kb == 0), stop=False))
                        b.on("tensor", t_body)

                    def t2_body(t, mg=mg):
                        for j in range(8):
                            pe.inc(t.matmul(pg[j][:, :], nhcol_bf[0:1, :], csum1[0:1, :],
                                            start=False, stop=True))
                    b.on("tensor", t2_body)

                    def s_body(s, mg=mg):
                        s.wait(pe)
                        for j in range(8):
                            sc.inc(s.activation(r2[:, (mg * 8 + j) * 512:(mg * 8 + j + 1) * 512],
                                                pg[j][:, :], AF.Relu, scale=vec4s1[:, 0:1]))
                    b.on("scalar", s_body)

            if dbg is not None and c == 0:
                def dmp2(s):
                    s.wait(sc)
                    s.wait(dv)
                    so.inc(s.dma_start(out=dbg["r2"], in_=r2[:, :]), 16)
                    so.inc(s.dma_start(out=dbg["cs1"], in_=csum1[0:1, :]), 16)
                b.on("sync", dmp2)

            with ExitStack() as sctx:
                with nc.psum_tensor(f"{px}pcs2_{c}", [1, 512], F32) as pcs2:
                    def t_body(t, pcs2=pcs2):
                        t.wait(sc)
                        for tt in range(32):
                            pe.inc(t.matmul(pcs2[:, :], onecol_bf[:, 0:1], r2[:, tt * 512:(tt + 1) * 512],
                                            start=(tt == 0), stop=(tt == 31)))
                    b.on("tensor", t_body)

                    def v_body(v, pcs2=pcs2):
                        v.wait(pe)
                        dv.inc(v.tensor_copy(csum2[0:1, :], pcs2[0:1, :]))
                    b.on("vector", v_body)

                if dbg is not None and c == 0:
                    def dmp3(s):
                        s.wait(dv)
                        so.inc(s.dma_start(out=dbg["cs2"], in_=csum2[0:1, :]), 16)
                    b.on("sync", dmp3)

                ps2 = [sctx.enter_context(nc.psum_tensor(f"{px}ps2_{c}_{st}", [128, 1024], F32))
                       for st in range(4)]
                for tt in range(32):
                    buf = wblk[tt % 2]

                    def g_body(g, tt=tt, buf=buf):
                        g.wait(pe)
                        sg.inc(g.dma_start(out=buf[:, :], in_=k2[tt * 128:(tt + 1) * 128, :]), 16)
                    b.on("gpsimd", g_body)

                    def t2_body(t, tt=tt, buf=buf):
                        t.wait(sg)
                        if tt == 0:
                            t.wait(dv)
                        for st in range(4):
                            for n in range(2):
                                pe.inc(t.matmul(ps2[st][:, n * 512:(n + 1) * 512],
                                                r2[:, tt * 512 + st * 128:tt * 512 + (st + 1) * 128],
                                                buf[:, n * 512:(n + 1) * 512],
                                                start=(tt == 0), stop=False))
                    b.on("tensor", t2_body)

                def t3_body(t):
                    for st in range(4):
                        for n in range(2):
                            pe.inc(t.matmul(ps2[st][:, n * 512:(n + 1) * 512],
                                            csum2[0:1, st * 128:(st + 1) * 128],
                                            nhrow_bf[0:1, n * 512:(n + 1) * 512],
                                            start=False, stop=True))
                b.on("tensor", t3_body)

                for st in range(4):
                    ob = outb[st % 2]
                    sob = b.ctr(f"sob{st % 2}")

                    def v2_body(v, st=st, ob=ob, sob=sob):
                        v.wait(pe)
                        v.wait(sob)
                        for n in range(2):
                            dv.inc(v.scalar_tensor_tensor(ob[:, n * 512:(n + 1) * 512],
                                                          ps2[st][:, n * 512:(n + 1) * 512],
                                                          vec4s2[:, 0:1],
                                                          x3c[:, st * 1024 + n * 512:st * 1024 + (n + 1) * 512],
                                                          ALU.mult, ALU.add))
                    b.on("vector", v2_body)

                    def s_body(s, st=st, ob=ob, c=c, sob=sob):
                        s.wait(dv)
                        gt = c * 4 + st
                        sob.inc(s.dma_start(out=out[gt * 128:(gt + 1) * 128, :], in_=ob[:, :]), 16)
                    b.on("sync", s_body)

        if dbg is not None:
            def dump(s):
                s.wait(so)
                so.inc(s.dma_start(out=dbg["w0p"], in_=w0p.ap()), 16)
            b.on("sync", dump)

        def fin(s):
            s.wait(so)
            for name in ("sob0", "sob1"):
                if name in b.c:
                    s.wait(b.c[name])
        b.on("sync", fin)
    return nc



def build_attn0(nc, names, env=None, px='a0_'):
    x0 = _dram(nc, names, "x", [S, D], F32, "ExternalInput")
    Wd = {w: _dram(nc, names, w, [D, D], F32, "ExternalInput")
          for w in ("wq", "wk", "wv", "wo")}
    out = _dram(nc, names, "out", [S, D], F32, "ExternalOutput")
    ln0h = nc.dram_tensor("ln0T_scratch", [8, 128, S], BF16, kind="Internal")
    yph = nc.dram_tensor("yp_scratch", [8, 128, S], F32, kind="Internal")
    dbg = None
    if names.get("dbg"):
        dbg = {k: nc.dram_tensor("dbg_" + k, shp, dt, kind="ExternalOutput").ap()
               for k, shp, dt in [("KTg", [128, 4 * 4096], BF16), ("Vg", [128, 32 * 520], BF16),
                                  ("qcb", [128, 4 * 512], BF16), ("attnoT", [128, 4 * 512], BF16),
                                  ("ab0", [128, 512], BF16), ("pv0", [65, 512], F32),
                                  ("WTwk", [128, 8192], BF16), ("ln0", [8, 128, 4096], BF16)]}

    masks_np = np.zeros((128, 4 * 512), dtype=BF)
    for p in range(4):
        for kl in range(128):
            q0 = kl + 128 * p
            if q0 < 512:
                masks_np[kl, p * 512 + q0:(p + 1) * 512] = 1.0

    with ExitStack() as ctx:
        e = ctx.enter_context
        ident = e(nc.sbuf_tensor(f"{px}ident", [128, 128], BF16))
        identf = e(nc.sbuf_tensor(f"{px}identf", [128, 128], F32))
        onesf = e(nc.sbuf_tensor(f"{px}onesf", [128, 64], F32))
        masks = e(nc.sbuf_tensor(f"{px}masks", [128, 4 * 512], BF16))
        WT = {w: e(nc.sbuf_tensor(f"{px}WT{w}", [128, 8192], BF16)) for w in ("wq", "wk", "wv", "wo")}
        x0t = [e(nc.sbuf_tensor(f"{px}x0t{i}", [128, 1024], F32)) for i in range(2)]
        stat = e(nc.sbuf_tensor(f"{px}stat", [128, 12], F32))
        mv = e(nc.sbuf_tensor(f"{px}mv", [128, 2], F32))
        rstd = e(nc.sbuf_tensor(f"{px}rstd", [128, 1], F32))
        lnt = e(nc.sbuf_tensor(f"{px}lnt", [128, 1024], BF16))

        if env is None:
            block = e(nc.Block())
            b = B(nc, ctx, block)
        else:
            block = env["block"]
            b = env["b"]
        sl = b.ctr("sl"); sg = b.ctr("sg"); pe = b.ctr("pe")
        dv = b.ctr("dv"); sc = b.ctr("sc"); sp = b.ctr("sp"); sy = b.ctr("sy")
        sob = [b.ctr("sob0"), b.ctr("sob1")]

        identh = nc.inline_tensor(np.eye(128, dtype=BF), f"{px}identnp")
        identfh = nc.inline_tensor(np.eye(128, dtype=np.float32), f"{px}identfnp")
        masksh = nc.inline_tensor(masks_np, f"{px}masksnp")

        def a0_sync(s):
            sl.inc(s.dma_start(out=ident[:, :], in_=identh.ap()), 16)
            sl.inc(s.dma_start(out=identf[:, :], in_=identfh.ap()), 16)
            sl.inc(s.dma_start(out=masks[:, :], in_=masksh.ap()), 16)
        b.on("sync", a0_sync)

        def a0_vec(v):
            dv.inc(v.memset(onesf[:, :], 1.0))
        b.on("vector", a0_vec)

        with ExitStack() as actx:
            wstage = actx.enter_context(nc.sbuf_tensor(f"{px}wstage", [128, 8192], BF16))
            ln0T = actx.enter_context(nc.sbuf_tensor(f"{px}ln0T", [128, 8 * 4096], BF16))
            for wi, w in enumerate(("wq", "wk", "wv", "wo")):
                def g_body(g, w=w, wi=wi):
                    if wi > 0:
                        g.wait(pe)
                    sg.inc(g.dma_start(out=wstage[:, :].rearrange("p (t i) -> p t i", t=8),
                                       in_=Wd[w].rearrange("(t p) i -> p t i", p=128)), 16)
                b.on("gpsimd", g_body)

                with nc.psum_tensor(f"{px}ptw_{w}", [128, 1024], BF16) as ptw:
                    for u in range(8):
                        def t_body(t, w=w, u=u):
                            t.wait(sg)
                            t.wait(sl)
                            if u > 0:
                                t.wait(dv)
                            for tt in range(8):
                                pe.inc(t.transpose(ptw[:, tt * 128:(tt + 1) * 128],
                                                   wstage[:, tt * 1024 + u * 128:tt * 1024 + (u + 1) * 128],
                                                   ident[:, :]))
                        b.on("tensor", t_body)

                        def v_body(v, w=w, u=u):
                            v.wait(pe)
                            dv.inc(v.tensor_copy(WT[w][:, u * 1024:(u + 1) * 1024], ptw[:, :]))
                        b.on("vector", v_body)

            for st in range(NT):
                xb = x0t[st % 2]

                def s_body(s, st=st, xb=xb):
                    s.wait(dv)
                    sl.inc(s.dma_start(out=xb[:, :], in_=x0[st * 128:(st + 1) * 128, :]), 16)
                b.on("sync", s_body)

                def v_body(v, st=st, xb=xb):
                    v.wait(sl)
                    v.wait(dv)
                    v.bn_stats(stat[:, 0:6], xb[:, 0:512])
                    dv.inc(v.bn_stats(stat[:, 6:12], xb[:, 512:1024]))
                    v.wait(dv)
                    dv.inc(v.bn_aggr(mv[:, :], stat[:, :]))
                    v.wait(dv)
                    dv.inc(v.tensor_scalar_add(rstd[:, 0:1], mv[:, 1:2], EPS))
                b.on("vector", v_body)

                def s2_body(s):
                    s.wait(dv)
                    sc.inc(s.activation(rstd[:, 0:1], rstd[:, 0:1], AF.Sqrt))
                b.on("scalar", s2_body)

                def v2_body(v, st=st, xb=xb):
                    v.wait(sc)
                    dv.inc(v.reciprocal(rstd[:, 0:1], rstd[:, 0:1]))
                    v.wait(dv)
                    dv.inc(v.tensor_scalar(lnt[:, :], xb[:, :], mv[:, 0:1], rstd[:, 0:1],
                                           ALU.subtract, ALU.mult))
                b.on("vector", v2_body)

                with nc.psum_tensor(f"{px}plt_{st}", [128, 1024], BF16) as plt:
                    def t_body(t, st=st):
                        t.wait(dv)
                        for it in range(8):
                            pe.inc(t.transpose(plt[:, it * 128:(it + 1) * 128],
                                               lnt[:, it * 128:(it + 1) * 128], ident[:, :]))
                    b.on("tensor", t_body)

                    def v3_body(v, st=st):
                        v.wait(pe)
                        for it in range(8):
                            i = v.tensor_copy(ln0T[:, it * 4096 + st * 128:it * 4096 + (st + 1) * 128],
                                              plt[:, it * 128:(it + 1) * 128])
                        dv.inc(i)
                    b.on("vector", v3_body)

            def spill(s):
                s.wait(dv)
                for it in range(8):
                    sp.inc(s.dma_start(out=ln0h.ap()[it, :, :], in_=ln0T[:, it * 4096:(it + 1) * 4096]), 16)
            b.on("sync", spill)

        KTg = e(nc.sbuf_tensor(f"{px}KTg", [128, 4 * 4096], BF16))
        Vg = e(nc.sbuf_tensor(f"{px}Vg", [128, 32 * 520], BF16))
        lncb = [e(nc.sbuf_tensor(f"{px}lncb{i}", [128, 8 * 512], BF16)) for i in range(2)]
        qcb = e(nc.sbuf_tensor(f"{px}qcb", [128, 4 * 512], BF16))
        ab = [e(nc.sbuf_tensor(f"{px}ab{i}", [128, 512], BF16)) for i in range(4)]
        brs = e(nc.sbuf_tensor(f"{px}brs", [128, 512], F32))
        brd = e(nc.sbuf_tensor(f"{px}brd", [64, 512], F32))
        attnoT = e(nc.sbuf_tensor(f"{px}attnoT", [128, 4 * 512], BF16))
        ybuf = [e(nc.sbuf_tensor(f"{px}ybuf{i}", [128, 512], F32)) for i in range(2)]
        y0b = [e(nc.sbuf_tensor(f"{px}y0b{i}", [128, 512], F32)) for i in range(2)]

        def ln_chunk_load(s, buf, n, extra=None):
            sl.inc(s.dma_start(out=buf[:, :].rearrange("p (it w) -> p it w", it=8),
                               in_=ln0h.ap()[:, :, n * 512:(n + 1) * 512].rearrange("it p w -> p it w")), 16)

        for gp in range(2):
            def v_init(v, gp=gp):
                v.wait(pe)
                v.wait(sp)
                dv.inc(v.memset(Vg[:, :], 1.0))
            b.on("vector", v_init)

            with ExitStack() as sctx:
                pkv = [sctx.enter_context(nc.psum_tensor(f"{px}pkv_{gp}_{i}", [128, 512], F32)) for i in range(2)]
                for n in range(8):
                    buf = lncb[n % 2]

                    def s_body(s, gp=gp, n=n, buf=buf):
                        s.wait(pe)
                        if gp == 0 and n == 0:
                            s.wait(sp)
                        ln_chunk_load(s, buf, n)
                    b.on("sync", s_body)

                    for j in range(4):
                        def t_body(t, gp=gp, n=n, j=j, buf=buf):
                            t.wait(sl)
                            t.wait(sc)
                            if n == 0 and j == 0:
                                t.wait(dv)
                            for it in range(8):
                                i = t.matmul(pkv[j % 2][:, :],
                                             WT["wk"][:, it * 1024 + gp * 512 + j * 128:it * 1024 + gp * 512 + (j + 1) * 128],
                                             buf[:, it * 512:(it + 1) * 512],
                                             start=(it == 0), stop=(it == 7))
                            pe.inc(i)
                        b.on("tensor", t_body)

                        def sc_body(s, gp=gp, n=n, j=j):
                            s.wait(pe)
                            sc.inc(s.activation(KTg[:, j * 4096 + n * 512:j * 4096 + (n + 1) * 512],
                                                pkv[j % 2][:, :], AF.Identity))
                        b.on("scalar", sc_body)

                    for m4 in range(4):
                        stile = n * 4 + m4

                        def t_body(t, gp=gp, n=n, m4=m4, buf=buf, stile=stile):
                            t.wait(sc)
                            for it in range(8):
                                i = t.matmul(pkv[m4 % 2][:, :],
                                             buf[:, it * 512 + m4 * 128:it * 512 + (m4 + 1) * 128],
                                             WT["wv"][:, it * 1024 + gp * 512:it * 1024 + (gp + 1) * 512],
                                             start=(it == 0), stop=(it == 7))
                            pe.inc(i)
                        b.on("tensor", t_body)

                        def v_body(v, gp=gp, stile=stile, m4=m4):
                            v.wait(pe)
                            dst = bass.AP(Vg, stile * 520, [[32 * 520, 128], [65, 8], [1, 64]])
                            src = bass.AP(pkv[m4 % 2], 0, [[512, 128], [64, 8], [1, 64]])
                            dv.inc(v.tensor_copy(dst, src))
                        b.on("vector", v_body)

            if dbg is not None and gp == 0:
                sdbg = b.ctr("sdbg")

                def d_kv(s):
                    s.wait(dv)
                    s.wait(sc)
                    sdbg.inc(s.dma_start(out=dbg["KTg"], in_=KTg[:, :]), 16)
                    sdbg.inc(s.dma_start(out=dbg["Vg"], in_=Vg[:, :]), 16)
                    sdbg.inc(s.dma_start(out=dbg["WTwk"], in_=WT["wk"][:, :]), 16)
                    sdbg.inc(s.dma_start(out=dbg["ln0"], in_=ln0h.ap()), 16)
                b.on("sync", d_kv)

                def d_bar(v):
                    v.wait(b.c["sdbg"])
                    dv.inc(v.memset(brs[0:1, 0:1], 0.0))
                b.on("vector", d_bar)

            for qc in range(8):
                buf = lncb[qc % 2]

                def s_body(s, gp=gp, qc=qc, buf=buf):
                    s.wait(pe)
                    ln_chunk_load(s, buf, qc)
                b.on("sync", s_body)

                with ExitStack() as sctx:
                    pq = sctx.enter_context(nc.psum_tensor(f"{px}pq_{gp}_{qc}", [128, 512], F32))
                    psab = [sctx.enter_context(nc.psum_tensor(f"{px}ps_{gp}_{qc}_{i}", [128, 512], F32))
                            for i in range(4)]
                    pv = [sctx.enter_context(nc.psum_tensor(f"{px}pv_{gp}_{qc}_{i}", [65, 512], F32))
                          for i in range(2)]
                    pbr = sctx.enter_context(nc.psum_tensor(f"{px}pbr_{gp}_{qc}", [64, 512], F32))

                    for j in range(4):
                        def t_q(t, gp=gp, qc=qc, j=j, buf=buf):
                            t.wait(sl)
                            t.wait(sc)
                            if j == 0:
                                t.wait(dv)
                            for it in range(8):
                                i = t.matmul(pq[:, :],
                                             WT["wq"][:, it * 1024 + gp * 512 + j * 128:it * 1024 + gp * 512 + (j + 1) * 128],
                                             buf[:, it * 512:(it + 1) * 512],
                                             start=(it == 0), stop=(it == 7))
                            pe.inc(i)
                        b.on("tensor", t_q)

                        def sc_q(s, gp=gp, qc=qc, j=j):
                            s.wait(pe)
                            sc.inc(s.activation(qcb[:, j * 512:(j + 1) * 512], pq[:, :], AF.Identity))
                        b.on("scalar", sc_q)

                    nkb = 4 * (qc + 1)
                    for j in range(4):
                        for kb in range(nkb):
                            pA = psab[2 * (kb % 2)]
                            pB = psab[2 * (kb % 2) + 1]
                            aA = ab[2 * (kb % 2)]
                            aB = ab[2 * (kb % 2) + 1]

                            def t_s(t, gp=gp, qc=qc, j=j, kb=kb, pA=pA, pB=pB):
                                t.wait(sc)
                                t.wait(dv)
                                t.matmul(pA[:, :], KTg[0:64, j * 4096 + kb * 128:j * 4096 + (kb + 1) * 128],
                                         qcb[0:64, j * 512:(j + 1) * 512], start=True, stop=True,
                                         tile_position=(0, 0))
                                pe.inc(t.matmul(pB[:, :], KTg[64:128, j * 4096 + kb * 128:j * 4096 + (kb + 1) * 128],
                                                qcb[64:128, j * 512:(j + 1) * 512], start=True, stop=True,
                                                tile_position=(64, 0)))
                            b.on("tensor", t_s)

                            def sc_s(s, gp=gp, qc=qc, j=j, kb=kb, pA=pA, pB=pB, aA=aA, aB=aB):
                                s.wait(pe)
                                s.wait(dv)
                                s.activation(aA[:, :], pA[:, :], AF.Exp, scale=0.125)
                                sc.inc(s.activation(aB[:, :], pB[:, :], AF.Exp, scale=0.125))
                            b.on("scalar", sc_s)

                            diag = (kb >= 4 * qc)
                            if diag:
                                p_ = kb - 4 * qc

                                def v_m(v, aA=aA, aB=aB, p_=p_):
                                    v.wait(sc)
                                    dv.inc(v.tensor_mul(aA[:, :], aA[:, :], masks[:, p_ * 512:(p_ + 1) * 512]))
                                    dv.inc(v.tensor_mul(aB[:, :], aB[:, :], masks[:, p_ * 512:(p_ + 1) * 512]))
                                b.on("vector", v_m)

                            def t_av(t, gp=gp, qc=qc, j=j, kb=kb, aA=aA, aB=aB, diag=diag, nkb=nkb):
                                if diag:
                                    t.wait(dv)
                                else:
                                    t.wait(sc)
                                t.matmul(pv[0][:, :], Vg[:, kb * 520 + 2 * j * 65:kb * 520 + 2 * j * 65 + 65],
                                         aA[:, :], start=(kb == 0), stop=(kb == nkb - 1))
                                pe.inc(t.matmul(pv[1][:, :], Vg[:, kb * 520 + (2 * j + 1) * 65:kb * 520 + (2 * j + 1) * 65 + 65],
                                                aB[:, :], start=(kb == 0), stop=(kb == nkb - 1)))
                            b.on("tensor", t_av)

                        for hh in range(2):
                            def v_n(v, gp=gp, qc=qc, j=j, hh=hh):
                                v.wait(pe)
                                dv.inc(v.reciprocal(brs[64:65, :], pv[hh][64:65, :]))
                            b.on("vector", v_n)

                            def t_n(t, gp=gp, qc=qc, j=j, hh=hh):
                                t.wait(dv)
                                pe.inc(t.matmul(pbr[:, :], onesf[64:65, 0:64], brs[64:65, :], start=True, stop=True))
                            b.on("tensor", t_n)

                            def v_n2(v, gp=gp, qc=qc, j=j, hh=hh):
                                v.wait(pe)
                                dv.inc(v.tensor_copy(brd[:, :], pbr[:, :]))
                                v.wait(dv)
                                base = j * 512
                                if hh == 0:
                                    dv.inc(v.tensor_mul(attnoT[0:64, base:base + 512], pv[0][0:64, :], brd[0:64, :]))
                                else:
                                    dv.inc(v.tensor_mul(attnoT[64:128, base:base + 512], pv[1][0:64, :], brd[0:64, :]))
                            b.on("vector", v_n2)

                    if dbg is not None and gp == 0 and qc == 0:
                        def d_at(s):
                            s.wait(dv)
                            sdbg = b.ctr("sdbg")
                            sdbg.inc(s.dma_start(out=dbg["qcb"], in_=qcb[:, :]), 16)
                            sdbg.inc(s.dma_start(out=dbg["attnoT"], in_=attnoT[:, :]), 16)
                        b.on("sync", d_at)

                        def d_bar2(v):
                            v.wait(b.c["sdbg"])
                            dv.inc(v.memset(brs[0:1, 0:1], 0.0))
                        b.on("vector", d_bar2)

                    for m in range(8):
                        yb = ybuf[m % 2]

                        def t_o(t, gp=gp, qc=qc, m=m):
                            t.wait(dv)
                            t.wait(sc)
                            for itv in range(4):
                                i = t.matmul(pq[:, :],
                                             WT["wo"][:, (4 * gp + itv) * 1024 + m * 128:(4 * gp + itv) * 1024 + (m + 1) * 128],
                                             attnoT[:, itv * 512:(itv + 1) * 512],
                                             start=(itv == 0), stop=(itv == 3))
                            pe.inc(i)
                        b.on("tensor", t_o)

                        if gp == 0:
                            def v_o(v, qc=qc, m=m, yb=yb):
                                v.wait(pe)
                                v.wait(sy)
                                dv.inc(v.tensor_copy(yb[:, :], pq[:, :]))
                            b.on("vector", v_o)

                            def s_o(s, qc=qc, m=m, yb=yb):
                                s.wait(dv)
                                sy.inc(s.dma_start(out=yph.ap()[m, :, qc * 512:(qc + 1) * 512], in_=yb[:, :]), 16)
                            b.on("sync", s_o)
                        else:
                            y0 = y0b[m % 2]

                            def s_o0(s, qc=qc, m=m, y0=y0):
                                s.wait(dv)
                                sy.inc(s.dma_start(out=y0[:, :], in_=yph.ap()[m, :, qc * 512:(qc + 1) * 512]), 16)
                            b.on("sync", s_o0)

                            def v_o(v, qc=qc, m=m, yb=yb, y0=y0):
                                v.wait(pe)
                                v.wait(sy)
                                v.wait(sob[m % 2])
                                dv.inc(v.tensor_add(yb[:, :], pq[:, :], y0[:, :]))
                            b.on("vector", v_o)

                            def s_o(s, qc=qc, m=m, yb=yb):
                                s.wait(dv)
                                sob[m % 2].inc(s.dma_start(out=yph.ap()[m, :, qc * 512:(qc + 1) * 512], in_=yb[:, :]), 16)
                            b.on("sync", s_o)

        ytb = [e(nc.sbuf_tensor(f"{px}ytb{i}", [128, 1024], F32)) for i in range(2)]
        outt = [e(nc.sbuf_tensor(f"{px}outt{i}", [128, 1024], F32)) for i in range(2)]
        soo = [b.ctr("soo0"), b.ctr("soo1")]
        with ExitStack() as sctx:
            pxt = [sctx.enter_context(nc.psum_tensor(f"{px}pxt_{i}", [128, 1024], F32)) for i in range(2)]
            for st in range(NT):
                xb = x0t[st % 2]
                yt = ytb[st % 2]
                ob = outt[st % 2]

                def s_body(s, st=st, xb=xb, yt=yt):
                    s.wait(dv)
                    if st < 2:
                        s.wait(sob[0]); s.wait(sob[1])
                    sl.inc(s.dma_start(out=xb[:, :], in_=x0[st * 128:(st + 1) * 128, :]), 16)
                    sl.inc(s.dma_start(out=yt[:, :].rearrange("p (m w) -> p m w", m=8),
                                       in_=yph.ap()[:, :, st * 128:(st + 1) * 128].rearrange("m p w -> p m w")), 16)
                b.on("sync", s_body)

                def t_body(t, st=st, yt=yt):
                    t.wait(sl)
                    t.wait(dv)
                    for m in range(8):
                        pe.inc(t.transpose(pxt[st % 2][:, m * 128:(m + 1) * 128],
                                           yt[:, m * 128:(m + 1) * 128], identf[:, :]))
                b.on("tensor", t_body)

                def v_body(v, st=st, xb=xb, ob=ob):
                    v.wait(pe)
                    v.wait(soo[st % 2])
                    dv.inc(v.tensor_add(ob[:, :], pxt[st % 2][:, :], xb[:, :]))
                b.on("vector", v_body)

                def s2_body(s, st=st, ob=ob):
                    s.wait(dv)
                    soo[st % 2].inc(s.dma_start(out=out[st * 128:(st + 1) * 128, :], in_=ob[:, :]), 16)
                b.on("sync", s2_body)

        def fin(s):
            s.wait(soo[0]); s.wait(soo[1])
        b.on("sync", fin)
    return nc



def build_moe(nc, names, mode, env=None, px='mo_'):
    x1 = _dram(nc, names, "x", [S, D], F32, "ExternalInput")
    gw = _dram(nc, names, "gate_w", [8, D], F32, "ExternalInput")
    lb = _dram(nc, names, "lb_bias", [1, 8], F32, "ExternalInput")
    if mode == "full":
        egw = _dram(nc, names, "egw", [8, 512, D], F32, "ExternalInput")
        euw = _dram(nc, names, "euw", [8, 512, D], F32, "ExternalInput")
        edw = _dram(nc, names, "edw", [8, D, 512], F32, "ExternalInput")
        sgw = _dram(nc, names, "sgw", [D, D], F32, "ExternalInput")
        suw = _dram(nc, names, "suw", [D, D], F32, "ExternalInput")
        sdw = _dram(nc, names, "sdw", [D, D], F32, "ExternalInput")
        Sst = _dram(nc, names, "Sstack", [8, 8], F32, "ExternalInput")
        Cst = _dram(nc, names, "Cstack", [8, 8], F32, "ExternalInput")
        out = _dram(nc, names, "out", [S, D], F32, "ExternalOutput")
        gwT_h = nc.dram_tensor("gwT_scratch", [8, 128, 4096], BF16, kind="Internal")
        uwT_h = nc.dram_tensor("uwT_scratch", [8, 128, 4096], BF16, kind="Internal")
        dwT_h = nc.dram_tensor("dwT_scratch", [8, 128, 4096], BF16, kind="Internal")
        sgwT_h = nc.dram_tensor("sgwT_scratch", [128, 8192], BF16, kind="Internal")
        suwT_h = nc.dram_tensor("suwT_scratch", [128, 8192], BF16, kind="Internal")
        sdwT_h = nc.dram_tensor("sdwT_scratch", [128, 8192], BF16, kind="Internal")
    else:
        part = _dram(nc, names, "out", [16, 1], F32, "ExternalOutput")

    iota_np = np.broadcast_to(np.arange(8, dtype=np.float32), (128, 8)).copy()

    with ExitStack() as ctx:
        e = ctx.enter_context
        ident = e(nc.sbuf_tensor(f"{px}ident", [128, 128], BF16))
        identf = e(nc.sbuf_tensor(f"{px}identf", [128, 128], F32))
        onesf = e(nc.sbuf_tensor(f"{px}onesf", [1, 128], F32))
        onescol = e(nc.sbuf_tensor(f"{px}onescol", [128, 1], F32))
        iot = e(nc.sbuf_tensor(f"{px}iot", [128, 8], F32))
        iop8 = e(nc.sbuf_tensor(f"{px}iop8", [128, 8], F32))
        gws = e(nc.sbuf_tensor(f"{px}gws", [128, 1024], F32))
        gwsum = e(nc.sbuf_tensor(f"{px}gwsum", [128, 1], F32))
        gwTs = e(nc.sbuf_tensor(f"{px}gwTs", [128, 64], F32))
        lbB = e(nc.sbuf_tensor(f"{px}lbB", [128, 8], F32))
        gwsumB = e(nc.sbuf_tensor(f"{px}gwsumB", [128, 8], F32))
        x1t = [e(nc.sbuf_tensor(f"{px}x1t{i}", [128, 1024], F32)) for i in range(2)]
        x1Tt = e(nc.sbuf_tensor(f"{px}x1Tt", [128, 1024], F32))
        stat = e(nc.sbuf_tensor(f"{px}stat", [128, 12], F32))
        meanA = e(nc.sbuf_tensor(f"{px}meanA", [128, NT], F32))
        rstdA = e(nc.sbuf_tensor(f"{px}rstdA", [128, NT], F32))
        mvt = e(nc.sbuf_tensor(f"{px}mvt", [128, 2], F32))
        rws = e(nc.sbuf_tensor(f"{px}rws", [8, 128], F32))
        t8a = e(nc.sbuf_tensor(f"{px}t8a", [128, 8], F32))
        t8b = e(nc.sbuf_tensor(f"{px}t8b", [128, 8], F32))
        scoresb = e(nc.sbuf_tensor(f"{px}scoresb", [128, 8], F32))
        w1 = e(nc.sbuf_tensor(f"{px}w1", [128, 1], F32))
        w2 = e(nc.sbuf_tensor(f"{px}w2", [128, 1], F32))
        i1 = e(nc.sbuf_tensor(f"{px}i1", [128, 1], F32))
        i2 = e(nc.sbuf_tensor(f"{px}i2", [128, 1], F32))
        m1 = e(nc.sbuf_tensor(f"{px}m1", [128, 8], F32))
        m2 = e(nc.sbuf_tensor(f"{px}m2", [128, 8], F32))
        s2t = e(nc.sbuf_tensor(f"{px}s2t", [128, 8], F32))
        winv = e(nc.sbuf_tensor(f"{px}winv", [128, 1], F32))
        tc = e(nc.sbuf_tensor(f"{px}tc", [128, 16], F32))
        coefA = e(nc.sbuf_tensor(f"{px}coefA", [128, NT * 8], F32))
        partsb = e(nc.sbuf_tensor(f"{px}partsb", [16, 1], F32))

        if env is None:
            block = e(nc.Block())
            b = B(nc, ctx, block)
        else:
            block = env["block"]
            b = env["b"]
        sl = b.ctr("sl"); sg = b.ctr("sg"); pe = b.ctr("pe")
        dv = b.ctr("dv"); sc = b.ctr("sc"); so = b.ctr("so"); sw = b.ctr("sw")

        identh = nc.inline_tensor(np.eye(128, dtype=BF), f"{px}identnp")
        identfh = nc.inline_tensor(np.eye(128, dtype=np.float32), f"{px}identfnp")
        ioth = nc.inline_tensor(iota_np, f"{px}iotanp")
        iop8h = nc.inline_tensor(iota_np + 8.0, f"{px}iop8np")

        def g0_vec0(v):
            v.memset(onesf[:, :], 1.0)
            v.memset(onescol[:, :], 1.0)
            dv.inc(v.memset(gws[:, :], 0.0))
        b.on("vector", g0_vec0)

        def g0_sync(s):
            sl.inc(s.dma_start(out=ident[:, :], in_=identh.ap()), 16)
            sl.inc(s.dma_start(out=identf[:, :], in_=identfh.ap()), 16)
            sl.inc(s.dma_start(out=iot[:, :], in_=ioth.ap()), 16)
            sl.inc(s.dma_start(out=iop8[:, :], in_=iop8h.ap()), 16)
            s.wait(dv)
            sl.inc(s.dma_start(out=gws[0:8, :], in_=gw), 16)
        b.on("sync", g0_sync)

        def g0_vec(v):
            v.wait(sl)
            dv.inc(v.reduce_sum(gwsum[0:8, :], gws[0:8, :], axis=AX.X))
        b.on("vector", g0_vec)

        with ExitStack() as g0ctx:
            pgt = g0ctx.enter_context(nc.psum_tensor(f"{px}pgt", [128, 128], F32))
            pr8 = g0ctx.enter_context(nc.psum_tensor(f"{px}pr8", [128, 8], F32))

            for it in range(8):
                def t_body(t, it=it):
                    t.wait(sl)
                    t.wait(dv)
                    if it > 0:
                        t.wait(dv)
                    pe.inc(t.transpose(pgt[:, :], gws[:, it * 128:(it + 1) * 128], identf[:, :]))
                b.on("tensor", t_body)

                def v_body(v, it=it):
                    v.wait(pe)
                    dv.inc(v.tensor_copy(gwTs[:, it * 8:(it + 1) * 8], pgt[:, 0:8]))
                b.on("vector", v_body)

            def t2_body(t):
                t.wait(dv)
                pe.inc(t.matmul(pr8[0:1, :], gwsum[0:8, 0:1], identf[0:8, 0:8], start=True, stop=True))
            b.on("tensor", t2_body)

            def v2_body(v):
                v.wait(pe)
                dv.inc(v.tensor_copy(t8a[0:1, :], pr8[0:1, :]))
            b.on("vector", v2_body)

            def s_body(s):
                sl.inc(s.dma_start(out=t8b[0:1, :], in_=lb), 16)
            b.on("sync", s_body)

            def t3_body(t):
                t.wait(dv)
                t.wait(sl)
                pe.inc(t.matmul(pr8[:, :], onesf[0:1, :], t8a[0:1, :], start=True, stop=True))
            b.on("tensor", t3_body)

            def v3_body(v):
                v.wait(pe)
                dv.inc(v.tensor_copy(gwsumB[:, :], pr8[:, :]))
            b.on("vector", v3_body)

            def t4_body(t):
                t.wait(dv)
                pe.inc(t.matmul(pr8[:, :], onesf[0:1, :], t8b[0:1, :], start=True, stop=True))
            b.on("tensor", t4_body)

            def v4_body(v):
                v.wait(pe)
                dv.inc(v.tensor_copy(lbB[:, :], pr8[:, :]))
            b.on("vector", v4_body)

            ppart = g0ctx.enter_context(nc.psum_tensor(f"{px}ppart", [16, 1], F32))
            pxt = g0ctx.enter_context(nc.psum_tensor(f"{px}pxt", [128, 1024], F32))
            for st in range(NT):
                xb = x1t[st % 2]

                def s_body(s, st=st, xb=xb):
                    s.wait(pe)
                    sl.inc(s.dma_start(out=xb[:, :], in_=x1[st * 128:(st + 1) * 128, :]), 16)
                b.on("sync", s_body)

                def v_body(v, st=st, xb=xb):
                    v.wait(sl)
                    v.wait(dv)
                    v.bn_stats(stat[:, 0:6], xb[:, 0:512])
                    dv.inc(v.bn_stats(stat[:, 6:12], xb[:, 512:1024]))
                    v.wait(dv)
                    dv.inc(v.bn_aggr(mvt[:, :], stat[:, :]))
                    v.wait(dv)
                    v.tensor_copy(meanA[:, st:st + 1], mvt[:, 0:1])
                    dv.inc(v.tensor_scalar_add(rstdA[:, st:st + 1], mvt[:, 1:2], EPS))
                b.on("vector", v_body)

                def sc_body(s, st=st):
                    s.wait(dv)
                    sc.inc(s.activation(rstdA[:, st:st + 1], rstdA[:, st:st + 1], AF.Sqrt))
                b.on("scalar", sc_body)

                def v2b(v, st=st):
                    v.wait(sc)
                    dv.inc(v.reciprocal(rstdA[:, st:st + 1], rstdA[:, st:st + 1]))
                b.on("vector", v2b)

                def t_body(t, st=st, xb=xb):
                    t.wait(sl)
                    t.wait(dv)
                    for it in range(8):
                        pe.inc(t.transpose(pxt[:, it * 128:(it + 1) * 128],
                                           xb[:, it * 128:(it + 1) * 128], identf[:, :]))
                b.on("tensor", t_body)

                def v3b(v, st=st):
                    v.wait(pe)
                    dv.inc(v.tensor_copy(x1Tt[:, :], pxt[:, :]))
                b.on("vector", v3b)

                def t2b(t, st=st):
                    t.wait(dv)
                    t.wait(sc)
                    for it in range(8):
                        i = t.matmul(pgt[0:8, 0:128], gwTs[:, it * 8:(it + 1) * 8],
                                     x1Tt[:, it * 128:(it + 1) * 128], start=(it == 0), stop=(it == 7))
                    pe.inc(i)
                b.on("tensor", t2b)

                def sc2b(s, st=st):
                    s.wait(pe)
                    sc.inc(s.copy(rws[:, :], pgt[0:8, 0:128]))
                b.on("scalar", sc2b)

                def t3b(t, st=st):
                    t.wait(sc)
                    t.wait(dv)
                    pe.inc(t.matmul(pr8[:, :], rws[:, :], identf[0:8, 0:8], start=True, stop=True))
                b.on("tensor", t3b)

                def v4b(v, st=st):
                    v.wait(pe)
                    dv.inc(v.tensor_scalar_mul(t8a[:, :], gwsumB[:, :], meanA[:, st:st + 1]))
                    v.wait(dv)
                    dv.inc(v.tensor_sub(t8b[:, :], pr8[:, :], t8a[:, :]))
                    v.wait(dv)
                    dv.inc(v.tensor_scalar_mul(t8b[:, :], t8b[:, :], rstdA[:, st:st + 1]))
                    v.wait(dv)
                    dv.inc(v.tensor_add(t8b[:, :], t8b[:, :], lbB[:, :]))
                b.on("vector", v4b)

                def sc3b(s, st=st):
                    s.wait(dv)
                    sc.inc(s.activation(scoresb[:, :], t8b[:, :], AF.Sigmoid))
                b.on("scalar", sc3b)

                def v5b(v, st=st):
                    v.wait(sc)
                    dv.inc(v.reduce_max(w1[:, :], scoresb[:, :], axis=AX.X))
                    v.wait(dv)
                    dv.inc(v.tensor_scalar(m1[:, :], scoresb[:, :], w1[:, 0:1], None, ALU.is_ge))
                    v.wait(dv)
                    dv.inc(v.scalar_tensor_tensor(t8a[:, :], m1[:, :], -8.0, iop8[:, :], ALU.mult, ALU.add))
                    v.wait(dv)
                    dv.inc(v.tensor_reduce(i1[:, :], t8a[:, :], axis=AX.X, op=ALU.min))
                    v.wait(dv)
                    dv.inc(v.tensor_scalar(m1[:, :], iot[:, :], i1[:, 0:1], None, ALU.is_equal))
                    v.wait(dv)
                    dv.inc(v.scalar_tensor_tensor(s2t[:, :], m1[:, :], -1e30, scoresb[:, :], ALU.mult, ALU.add))
                    v.wait(dv)
                    dv.inc(v.reduce_max(w2[:, :], s2t[:, :], axis=AX.X))
                    v.wait(dv)
                    dv.inc(v.tensor_scalar(m2[:, :], s2t[:, :], w2[:, 0:1], None, ALU.is_ge))
                    v.wait(dv)
                    dv.inc(v.scalar_tensor_tensor(t8a[:, :], m2[:, :], -8.0, iop8[:, :], ALU.mult, ALU.add))
                    v.wait(dv)
                    dv.inc(v.tensor_reduce(i2[:, :], t8a[:, :], axis=AX.X, op=ALU.min))
                    v.wait(dv)
                    dv.inc(v.tensor_scalar(m2[:, :], iot[:, :], i2[:, 0:1], None, ALU.is_equal))
                    v.wait(dv)
                    dv.inc(v.tensor_add(winv[:, :], w1[:, :], w2[:, :]))
                    v.wait(dv)
                    dv.inc(v.tensor_scalar_add(winv[:, :], winv[:, :], 1e-8))
                    v.wait(dv)
                    dv.inc(v.reciprocal(winv[:, :], winv[:, :]))
                    v.wait(dv)
                    dv.inc(v.tensor_scalar_mul(t8a[:, :], m1[:, :], w1[:, 0:1]))
                    v.wait(dv)
                    dv.inc(v.tensor_scalar_mul(t8b[:, :], m2[:, :], w2[:, 0:1]))
                    v.wait(dv)
                    dv.inc(v.tensor_add(t8a[:, :], t8a[:, :], t8b[:, :]))
                    v.wait(dv)
                    dv.inc(v.tensor_scalar_mul(tc[:, 0:8], t8a[:, :], winv[:, 0:1]))
                    v.wait(dv)
                    dv.inc(v.tensor_add(tc[:, 8:16], m1[:, :], m2[:, :]))
                    v.wait(dv)
                    dv.inc(v.tensor_copy(coefA[:, st * 8:(st + 1) * 8], tc[:, 8:16]))
                b.on("vector", v5b)

                def t4b(t, st=st):
                    t.wait(dv)
                    pe.inc(t.matmul(ppart[:, :], tc[:, :], onescol[:, :],
                                    start=(st == 0), stop=(st == NT - 1)))
                b.on("tensor", t4b)

            def v6b(v):
                v.wait(pe)
                dv.inc(v.tensor_copy(partsb[:, :], ppart[:, :]))
            b.on("vector", v6b)

            if mode == "partials":
                def s_out(s):
                    s.wait(dv)
                    so.inc(s.dma_start(out=part, in_=partsb[:, :]), 16)
                    s.wait(so)
                b.on("sync", s_out)
                return nc

            wmB = e(nc.sbuf_tensor(f"{px}wmB", [128, 8], F32))
            Ssb = e(nc.sbuf_tensor(f"{px}Ssb", [8, 8], F32))
            Csb = e(nc.sbuf_tensor(f"{px}Csb", [8, 8], F32))
            Sv = e(nc.sbuf_tensor(f"{px}Sv", [8, 1], F32))
            Cv = e(nc.sbuf_tensor(f"{px}Cv", [8, 1], F32))

            def s2_body(s):
                sl.inc(s.dma_start(out=Ssb[:, :], in_=Sst), 16)
                sl.inc(s.dma_start(out=Csb[:, :], in_=Cst), 16)
            b.on("sync", s2_body)

            def v7b(v):
                v.wait(sl)
                v.wait(dv)
                dv.inc(v.reduce_sum(Sv[:, :], Ssb[:, :], axis=AX.X))
                dv.inc(v.reduce_sum(Cv[:, :], Csb[:, :], axis=AX.X))
                v.wait(dv)
                dv.inc(v.tensor_scalar_max(Cv[:, :], Cv[:, :], 1.0))
                v.wait(dv)
                dv.inc(v.reciprocal(Cv[:, :], Cv[:, :]))
                v.wait(dv)
                dv.inc(v.tensor_mul(Sv[:, :], Sv[:, :], Cv[:, :]))
            b.on("vector", v7b)

            def t5b(t):
                t.wait(dv)
                pe.inc(t.matmul(pr8[0:1, :], Sv[:, 0:1], identf[0:8, 0:8], start=True, stop=True))
            b.on("tensor", t5b)

            def v8b(v):
                v.wait(pe)
                dv.inc(v.tensor_copy(t8a[0:1, :], pr8[0:1, :]))
            b.on("vector", v8b)

            def t6b(t):
                t.wait(dv)
                pe.inc(t.matmul(pr8[:, :], onesf[0:1, :], t8a[0:1, :], start=True, stop=True))
            b.on("tensor", t6b)

            def v9b(v):
                v.wait(pe)
                dv.inc(v.tensor_copy(wmB[:, :], pr8[:, :]))
                v.wait(dv)
                for st in range(NT):
                    i = v.tensor_mul(coefA[:, st * 8:(st + 1) * 8], coefA[:, st * 8:(st + 1) * 8], wmB[:, :])
                dv.inc(i)
            b.on("vector", v9b)

        with ExitStack() as wctx:
            wstage = wctx.enter_context(nc.sbuf_tensor(f"{px}wstage", [128, 8192], BF16))
            wTsb = wctx.enter_context(nc.sbuf_tensor(f"{px}wTsb", [128, 8192], BF16))
            ptw = wctx.enter_context(nc.psum_tensor(f"{px}ptw", [128, 1024], BF16))
            jobs = []
            for ee in range(8):
                jobs += [(egw[ee], gwT_h.ap()[ee], 4), (euw[ee], uwT_h.ap()[ee], 4),
                         (edw[ee], dwT_h.ap()[ee], 8)]
            jobs += [(sgw, sgwT_h.ap(), 8), (suw, suwT_h.ap(), 8), (sdw, sdwT_h.ap(), 8)]
            for ji, (src, dsth, nt) in enumerate(jobs):
                ncols = 1024 if nt == 8 else 1024
                ncols = src.shape[-1]

                def g_body(g, ji=ji, src=src, nt=nt, ncols=ncols):
                    g.wait(pe)
                    sg.inc(g.dma_start(
                        out=wstage[:, 0:nt * ncols].rearrange("p (t i) -> p t i", t=nt),
                        in_=src.rearrange("(t p) i -> p t i", p=128)), 16)
                b.on("gpsimd", g_body)

                nu = ncols // 128
                for u in range(nu):
                    def t_body(t, ji=ji, u=u, nt=nt, ncols=ncols):
                        t.wait(sg)
                        t.wait(sl)
                        t.wait(dv)
                        for tt in range(nt):
                            pe.inc(t.transpose(ptw[:, tt * 128:(tt + 1) * 128],
                                               wstage[:, tt * ncols + u * 128:tt * ncols + (u + 1) * 128],
                                               ident[:, :]))
                    b.on("tensor", t_body)

                    def v_body(v, ji=ji, u=u, nt=nt):
                        v.wait(pe)
                        v.wait(sw)
                        dv.inc(v.tensor_copy(wTsb[:, u * nt * 128:(u + 1) * nt * 128], ptw[:, 0:nt * 128]))
                    b.on("vector", v_body)

                def s_body(s, ji=ji, dsth=dsth, nt=nt, ncols=ncols):
                    s.wait(dv)
                    nu_ = ncols // 128
                    sw.inc(s.dma_start(out=dsth, in_=wTsb[:, 0:nu_ * nt * 128]), 16)
                b.on("sync", s_body)

        acc = e(nc.sbuf_tensor(f"{px}acc", [128, 16 * 1024], F32))
        ln1T = e(nc.sbuf_tensor(f"{px}ln1T", [128, 8 * 2048], BF16))
        lnt = e(nc.sbuf_tensor(f"{px}lnt", [128, 1024], BF16))
        wgs = e(nc.sbuf_tensor(f"{px}wgs", [128, 8192], BF16))
        wus = e(nc.sbuf_tensor(f"{px}wus", [128, 8192], BF16))
        wds = e(nc.sbuf_tensor(f"{px}wds", [128, 8192], BF16))
        silb = e(nc.sbuf_tensor(f"{px}silb", [128, 512], BF16))
        actb = e(nc.sbuf_tensor(f"{px}actb", [128, 8 * 512], BF16))

        for h in range(2):
            def s_body(s, h=h):
                s.wait(dv)
                s.wait(so)
                s.wait(sw)
                sl.inc(s.dma_start(out=acc[:, :].rearrange("p (t d) -> p t d", t=16),
                                   in_=x1[h * 2048:(h + 1) * 2048, :].rearrange("(t p) d -> p t d", p=128)), 16)
            b.on("sync", s_body)

            with nc.psum_tensor(f"{px}plt_{h}", [128, 1024], BF16) as plt:
                for stt in range(16):
                    gst = h * 16 + stt

                    def v_body(v, h=h, stt=stt, gst=gst):
                        v.wait(sl)
                        v.wait(pe)
                        dv.inc(v.tensor_scalar(lnt[:, :], acc[:, stt * 1024:(stt + 1) * 1024],
                                               meanA[:, gst:gst + 1], rstdA[:, gst:gst + 1],
                                               ALU.subtract, ALU.mult))
                    b.on("vector", v_body)

                    def t_body(t, h=h, stt=stt):
                        t.wait(dv)
                        for it in range(8):
                            pe.inc(t.transpose(plt[:, it * 128:(it + 1) * 128],
                                               lnt[:, it * 128:(it + 1) * 128], ident[:, :]))
                    b.on("tensor", t_body)

                    def v2_body(v, h=h, stt=stt):
                        v.wait(pe)
                        for it in range(8):
                            i = v.tensor_copy(ln1T[:, it * 2048 + stt * 128:it * 2048 + (stt + 1) * 128],
                                              plt[:, it * 128:(it + 1) * 128])
                        dv.inc(i)
                    b.on("vector", v2_body)

            for ee in range(9):
                shared = (ee == 8)
                if shared:
                    def g_body(g, h=h):
                        g.wait(pe)
                        g.wait(sw)
                        sg.inc(g.dma_start(out=wgs[:, :], in_=sgwT_h.ap()), 16)
                        sg.inc(g.dma_start(out=wus[:, :], in_=suwT_h.ap()), 16)
                        sg.inc(g.dma_start(out=wds[:, :], in_=sdwT_h.ap()), 16)
                    b.on("gpsimd", g_body)
                else:
                    def g_body(g, h=h, ee=ee):
                        g.wait(pe)
                        g.wait(sw)
                        sg.inc(g.dma_start(out=wgs[:, 0:4096], in_=gwT_h.ap()[ee]), 16)
                        sg.inc(g.dma_start(out=wus[:, 0:4096], in_=uwT_h.ap()[ee]), 16)
                        sg.inc(g.dma_start(out=wds[:, 0:4096], in_=dwT_h.ap()[ee]), 16)
                    b.on("gpsimd", g_body)

                nI = 8 if shared else 4
                wgc, wuc, wdc = wgs, wus, wds
                for sc_ in range(4):
                    for Ih in range(nI // 4):
                        with ExitStack() as pctx:
                            pgu = [pctx.enter_context(nc.psum_tensor(f"{px}pgu_{h}_{ee}_{sc_}_{Ih}_{i}", [128, 512], F32))
                                   for i in range(8)]

                            def t_body(t, h=h, ee=ee, sc_=sc_, Ih=Ih, wgc=wgc, wuc=wuc, nI=nI):
                                t.wait(sg)
                                t.wait(dv)
                                t.wait(sc)
                                for It4 in range(4):
                                    It = Ih * 4 + It4
                                    for it in range(8):
                                        t.matmul(pgu[It4][:, :],
                                                 wgc[:, it * (nI * 128) + It * 128:it * (nI * 128) + (It + 1) * 128],
                                                 ln1T[:, it * 2048 + sc_ * 512:it * 2048 + (sc_ + 1) * 512],
                                                 start=(it == 0), stop=(it == 7))
                                    for it in range(8):
                                        i = t.matmul(pgu[4 + It4][:, :],
                                                     wuc[:, it * (nI * 128) + It * 128:it * (nI * 128) + (It + 1) * 128],
                                                     ln1T[:, it * 2048 + sc_ * 512:it * 2048 + (sc_ + 1) * 512],
                                                     start=(it == 0), stop=(it == 7))
                                pe.inc(i)
                            b.on("tensor", t_body)

                            for It4 in range(4):
                                def sc_body(s, It4=It4):
                                    s.wait(pe)
                                    s.wait(dv)
                                    sc.inc(s.activation(silb[:, :], pgu[It4][:, :], AF.Silu))
                                b.on("scalar", sc_body)

                                def v_body(v, It4=It4, Ih=Ih):
                                    v.wait(sc)
                                    dv.inc(v.tensor_mul(actb[:, (Ih * 4 + It4) * 512:(Ih * 4 + It4 + 1) * 512],
                                                        silb[:, :], pgu[4 + It4][:, :]))
                                b.on("vector", v_body)

                    with ExitStack() as pctx:
                        pd = [pctx.enter_context(nc.psum_tensor(f"{px}pd_{h}_{ee}_{sc_}_{i}", [128, 1024], F32))
                              for i in range(2)]
                        for stt4 in range(4):
                            gst = h * 16 + sc_ * 4 + stt4

                            def t_body(t, h=h, ee=ee, sc_=sc_, stt4=stt4, wdc=wdc, nI=nI):
                                t.wait(dv)
                                for It in range(nI):
                                    for n in range(2):
                                        i = t.matmul(pd[stt4 % 2][:, n * 512:(n + 1) * 512],
                                                     actb[:, It * 512 + stt4 * 128:It * 512 + (stt4 + 1) * 128],
                                                     wdc[:, It * 1024 + n * 512:It * 1024 + (n + 1) * 512],
                                                     start=(It == 0), stop=(It == nI - 1))
                                pe.inc(i)
                            b.on("tensor", t_body)

                            def v_body(v, h=h, ee=ee, sc_=sc_, stt4=stt4, gst=gst, shared=shared):
                                v.wait(pe)
                                stt = sc_ * 4 + stt4
                                coef = 1.0 if shared else coefA[:, gst * 8 + ee:gst * 8 + ee + 1]
                                dv.inc(v.scalar_tensor_tensor(acc[:, stt * 1024:(stt + 1) * 1024],
                                                              pd[stt4 % 2][:, :], coef,
                                                              acc[:, stt * 1024:(stt + 1) * 1024],
                                                              ALU.mult, ALU.add))
                            b.on("vector", v_body)

            def s2_body(s, h=h):
                s.wait(dv)
                so.inc(s.dma_start(out=out[h * 2048:(h + 1) * 2048, :].rearrange("(t p) d -> p t d", p=128),
                                   in_=acc[:, :].rearrange("p (t d) -> p t d", t=16)), 16)
            b.on("sync", s2_body)

        def fin(s):
            s.wait(so)
        b.on("sync", fin)
    return nc



def build_attn1(nc, names, env=None, px='a1_'):
    x2 = _dram(nc, names, "x", [S, D], F32, "ExternalInput")
    wnames = ["p", "q", "k", "v", "o"]
    Wd = {w: _dram(nc, names, "W" + w, [D, D], F32, "ExternalInput") for w in wnames}
    Bd = {w: _dram(nc, names, "b" + w, [1, D], F32, "ExternalInput") for w in wnames}
    out = _dram(nc, names, "out", [S, D], F32, "ExternalOutput")
    ln2h = nc.dram_tensor("ln2T_scratch", [8, 128, S + 2], BF16, kind="Internal")

    bsum_np = np.zeros((128, 8 * 16), dtype=BF)
    bbc_np = np.zeros((16, 8 * 128), dtype=BF)
    for t in range(8):
        bsum_np[0:64, t * 16 + 2 * t] = 1
        bsum_np[64:128, t * 16 + 2 * t + 1] = 1
        bbc_np[2 * t, t * 128:t * 128 + 64] = 1
        bbc_np[2 * t + 1, t * 128 + 64:t * 128 + 128] = 1

    with ExitStack() as ctx:
        e = ctx.enter_context
        ident = e(nc.sbuf_tensor(f"{px}ident", [128, 128], BF16))
        identf = e(nc.sbuf_tensor(f"{px}identf", [128, 128], F32))
        bsum = e(nc.sbuf_tensor(f"{px}bsum", [128, 8 * 16], BF16))
        bbc = e(nc.sbuf_tensor(f"{px}bbc", [16, 8 * 128], BF16))
        WT = {w: e(nc.sbuf_tensor(f"{px}WT{w}", [128, 8192], BF16)) for w in wnames}
        bcol = {w: e(nc.sbuf_tensor(f"{px}bcol{w}", [128, 8], F32)) for w in wnames}
        x2t = [e(nc.sbuf_tensor(f"{px}x2t{i}", [128, 1024], F32)) for i in range(2)]
        stat = e(nc.sbuf_tensor(f"{px}stat", [128, 12], F32))
        mv = e(nc.sbuf_tensor(f"{px}mv", [128, 2], F32))
        rstd = e(nc.sbuf_tensor(f"{px}rstd", [128, 1], F32))
        lnt = e(nc.sbuf_tensor(f"{px}lnt", [128, 1024], BF16))
        d1ctx = ExitStack()
        wstage = d1ctx.enter_context(nc.sbuf_tensor(f"{px}wstage", [128, 8192], BF16))
        ln2T = d1ctx.enter_context(nc.sbuf_tensor(f"{px}ln2T", [128, 8 * 4096], BF16))
        if env is None:
            block = e(nc.Block())
            b = B(nc, ctx, block)
        else:
            block = env["block"]
            b = env["b"]
        sl = b.ctr("sl"); sg = b.ctr("sg"); pe = b.ctr("pe")
        dv = b.ctr("dv"); sc = b.ctr("sc"); sp = b.ctr("sp")
        sob = [b.ctr("sob0"), b.ctr("sob1")]

        identh = nc.inline_tensor(np.eye(128, dtype=BF), f"{px}identnp")
        identfh = nc.inline_tensor(np.eye(128, dtype=np.float32), f"{px}identfnp")
        bsumh = nc.inline_tensor(bsum_np, f"{px}bsumnp")
        bbch = nc.inline_tensor(bbc_np, f"{px}bbcnp")

        def d0_sync(s):
            sl.inc(s.dma_start(out=ident[:, :], in_=identh.ap()), 16)
            sl.inc(s.dma_start(out=identf[:, :], in_=identfh.ap()), 16)
            sl.inc(s.dma_start(out=bsum[:, :], in_=bsumh.ap()), 16)
            sl.inc(s.dma_start(out=bbc[:, :], in_=bbch.ap()), 16)
            for w in wnames:
                for m in range(8):
                    sl.inc(s.dma_start(out=bcol[w][:, m:m + 1],
                                       in_=Bd[w][0:1, m * 128:(m + 1) * 128].rearrange("o p -> p o")), 16)
        b.on("sync", d0_sync)

        def d0_vec(v):
            v.wait(sl)
            dv.inc(v.tensor_scalar_mul(bcol["q"][:, :], bcol["q"][:, :], 0.125))
        b.on("vector", d0_vec)

        for wi, w in enumerate(wnames):
            def g_body(g, w=w, wi=wi):
                if wi > 0:
                    g.wait(pe)
                sg.inc(g.dma_start(out=wstage[:, :].rearrange("p (t i) -> p t i", t=8),
                                   in_=Wd[w].rearrange("(t p) i -> p t i", p=128)), 16)
            b.on("gpsimd", g_body)

            with nc.psum_tensor(f"{px}ptw_{w}", [128, 1024], BF16) as ptw:
                for u in range(8):
                    def t_body(t, w=w, u=u):
                        t.wait(sg)
                        t.wait(sl)
                        if u > 0:
                            t.wait(dv)
                        for tt in range(8):
                            pe.inc(t.transpose(ptw[:, tt * 128:(tt + 1) * 128],
                                               wstage[:, tt * 1024 + u * 128:tt * 1024 + (u + 1) * 128],
                                               ident[:, :]))
                    b.on("tensor", t_body)

                    def v_body(v, w=w, u=u):
                        v.wait(pe)
                        dv.inc(v.tensor_copy(WT[w][:, u * 1024:(u + 1) * 1024], ptw[:, :]))
                    b.on("vector", v_body)

        for st in range(NT):
            xb = x2t[st % 2]

            def s_body(s, st=st, xb=xb):
                s.wait(dv)
                sl.inc(s.dma_start(out=xb[:, :], in_=x2[st * 128:(st + 1) * 128, :]), 16)
            b.on("sync", s_body)

            def v_body(v, st=st, xb=xb):
                v.wait(sl)
                v.wait(dv)
                v.bn_stats(stat[:, 0:6], xb[:, 0:512])
                dv.inc(v.bn_stats(stat[:, 6:12], xb[:, 512:1024]))
                v.wait(dv)
                dv.inc(v.bn_aggr(mv[:, :], stat[:, :]))
                v.wait(dv)
                dv.inc(v.tensor_scalar_add(rstd[:, 0:1], mv[:, 1:2], EPS))
            b.on("vector", v_body)

            def s2_body(s):
                s.wait(dv)
                sc.inc(s.activation(rstd[:, 0:1], rstd[:, 0:1], AF.Sqrt))
            b.on("scalar", s2_body)

            def v2_body(v, st=st, xb=xb):
                v.wait(sc)
                dv.inc(v.reciprocal(rstd[:, 0:1], rstd[:, 0:1]))
                v.wait(dv)
                dv.inc(v.tensor_scalar(lnt[:, :], xb[:, :], mv[:, 0:1], rstd[:, 0:1],
                                       ALU.subtract, ALU.mult))
            b.on("vector", v2_body)

            with nc.psum_tensor(f"{px}plt_{st}", [128, 1024], BF16) as plt:
                def t_body(t, st=st):
                    t.wait(dv)
                    for it in range(8):
                        pe.inc(t.transpose(plt[:, it * 128:(it + 1) * 128],
                                           lnt[:, it * 128:(it + 1) * 128], ident[:, :]))
                b.on("tensor", t_body)

                def v3_body(v, st=st):
                    v.wait(pe)
                    for it in range(8):
                        i = v.tensor_copy(ln2T[:, it * 4096 + st * 128:it * 4096 + (st + 1) * 128],
                                          plt[:, it * 128:(it + 1) * 128])
                    dv.inc(i)
                b.on("vector", v3_body)

        def spill(s):
            s.wait(dv)
            for it in range(8):
                sp.inc(s.dma_start(out=ln2h.ap()[it, :, 1:S + 1], in_=ln2T[:, it * 4096:(it + 1) * 4096]), 16)
                sp.inc(s.dma_start(out=ln2h.ap()[it, :, 0:1], in_=ln2T[:, it * 4096 + 4095:it * 4096 + 4096],
                                   allow_slow_non_contiguous=True), 16)
                sp.inc(s.dma_start(out=ln2h.ap()[it, :, S + 1:S + 2], in_=ln2T[:, it * 4096:it * 4096 + 1],
                                   allow_slow_non_contiguous=True), 16)
        b.on("sync", spill)
        d1ctx.close()

        lnc = e(nc.sbuf_tensor(f"{px}lnc", [128, 8 * 514], BF16))
        xpc = e(nc.sbuf_tensor(f"{px}xpc", [128, 8 * 514], BF16))
        qc = e(nc.sbuf_tensor(f"{px}qc", [128, 8 * 512], BF16))
        kc = e(nc.sbuf_tensor(f"{px}kc", [128, 8 * 514], BF16))
        vc = e(nc.sbuf_tensor(f"{px}vc", [128, 8 * 514], BF16))
        prodb = [e(nc.sbuf_tensor(f"{px}prodb{i}", [128, 512], BF16)) for i in range(2)]
        expb = e(nc.sbuf_tensor(f"{px}expb", [16, 3 * 512], F32))
        dnm = e(nc.sbuf_tensor(f"{px}dnm", [16, 512], F32))
        attnb = e(nc.sbuf_tensor(f"{px}attnb", [16, 3 * 512], BF16))
        acc = e(nc.sbuf_tensor(f"{px}acc", [128, 8 * 512], F32))
        tmpb = e(nc.sbuf_tensor(f"{px}tmpb", [128, 512], F32))
        attnoT = e(nc.sbuf_tensor(f"{px}attnoT", [128, 8 * 512], BF16))
        yT = e(nc.sbuf_tensor(f"{px}yT", [128, 8 * 512], F32))
        outt = [e(nc.sbuf_tensor(f"{px}outt{i}", [128, 1024], F32)) for i in range(2)]

        def mm_proj(t, w, m, dst_main, dst_halo, src, width):
            for it in range(8):
                i = t.matmul(dst_main[:, :], WT[w][:, it * 1024 + m * 128:it * 1024 + (m + 1) * 128],
                             src[:, it * 514 + 1:it * 514 + 513],
                             start=(it == 0), stop=(it == 7))
            if width == 514:
                for it in range(8):
                    i = t.matmul(dst_halo[:, 0:2], WT[w][:, it * 1024 + m * 128:it * 1024 + (m + 1) * 128],
                                 src[:, it * 514:it * 514 + 514:513],
                                 start=(it == 0), stop=(it == 7))
            return i

        for c in range(8):
            def s_body(s, c=c):
                s.wait(pe)
                if c == 0:
                    s.wait(sp)
                sl.inc(s.dma_start(out=lnc[:, :].rearrange("p (it w) -> p it w", it=8),
                                   in_=ln2h.ap()[:, :, c * 512:c * 512 + 514].rearrange("it p w -> p it w")), 16)
            b.on("sync", s_body)

            with ExitStack() as sctx:
                pmain = [sctx.enter_context(nc.psum_tensor(f"{px}pm_{c}_{i}", [128, 512], F32)) for i in range(2)]
                phalo = [sctx.enter_context(nc.psum_tensor(f"{px}phh_{c}_{i}", [128, 2], F32)) for i in range(2)]
                for m in range(8):
                    def t_body(t, c=c, m=m):
                        t.wait(sl)
                        t.wait(dv)
                        t.wait(sc)
                        pe.inc(mm_proj(t, "p", m, pmain[m % 2], phalo[m % 2], lnc, 514))
                    b.on("tensor", t_body)

                    def sc_body(s, c=c, m=m):
                        s.wait(pe)
                        s.wait(dv)
                        s.activation(xpc[:, m * 514 + 1:m * 514 + 513], pmain[m % 2][:, :], AF.Identity,
                                     bias=bcol["p"][:, m:m + 1])
                        i = s.activation(xpc[:, m * 514:m * 514 + 514:513], phalo[m % 2][:, :], AF.Identity,
                                         bias=bcol["p"][:, m:m + 1])
                        sc.inc(i)
                    b.on("scalar", sc_body)

                for m in range(8):
                    def t_body(t, c=c, m=m):
                        t.wait(sc)
                        pe.inc(mm_proj(t, "q", m, pmain[m % 2], None, xpc, 512))
                    b.on("tensor", t_body)

                    def sc_body(s, c=c, m=m):
                        s.wait(pe)
                        sc.inc(s.activation(qc[:, m * 512:(m + 1) * 512], pmain[m % 2][:, :], AF.Identity,
                                            bias=bcol["q"][:, m:m + 1], scale=0.125))
                    b.on("scalar", sc_body)

                for dst, w in ((kc, "k"), (vc, "v")):
                    for m in range(8):
                        def t_body(t, c=c, m=m, w=w):
                            t.wait(sc)
                            pe.inc(mm_proj(t, w, m, pmain[m % 2], phalo[m % 2], xpc, 514))
                        b.on("tensor", t_body)

                        def sc_body(s, c=c, m=m, w=w, dst=dst):
                            s.wait(pe)
                            s.activation(dst[:, m * 514 + 1:m * 514 + 513], pmain[m % 2][:, :], AF.Identity,
                                         bias=bcol[w][:, m:m + 1])
                            sc.inc(s.activation(dst[:, m * 514:m * 514 + 514:513], phalo[m % 2][:, :],
                                                AF.Identity, bias=bcol[w][:, m:m + 1]))
                        b.on("scalar", sc_body)

            with ExitStack() as sctx:
                psc = sctx.enter_context(nc.psum_tensor(f"{px}psc_{c}", [16, 3 * 512], F32))
                for n in range(3):
                    for t8 in range(8):
                        pb_ = prodb[(n * 8 + t8) % 2]

                        def v_body(v, c=c, n=n, t8=t8, pb_=pb_):
                            v.wait(sc)
                            v.wait(pe)
                            dv.inc(v.tensor_mul(pb_[:, :], qc[:, t8 * 512:(t8 + 1) * 512],
                                                kc[:, t8 * 514 + n:t8 * 514 + n + 512]))
                        b.on("vector", v_body)

                        def t_body(t, c=c, n=n, t8=t8, pb_=pb_):
                            t.wait(dv)
                            pe.inc(t.matmul(psc[:, n * 512:(n + 1) * 512], bsum[:, t8 * 16:(t8 + 1) * 16],
                                            pb_[:, :], start=(t8 == 0), stop=(t8 == 7)))
                        b.on("tensor", t_body)

                def sc_body(s, c=c):
                    s.wait(pe)
                    sc.inc(s.activation(expb[:, :], psc[:, :], AF.Exp))
                b.on("scalar", sc_body)

                def v_body(v, c=c):
                    v.wait(sc)
                    dv.inc(v.tensor_add(dnm[:, :], expb[:, 0:512], expb[:, 512:1024]))
                    v.wait(dv)
                    dv.inc(v.tensor_add(dnm[:, :], dnm[:, :], expb[:, 1024:1536]))
                    v.wait(dv)
                    dv.inc(v.reciprocal(dnm[:, :], dnm[:, :]))
                    v.wait(dv)
                    for n in range(3):
                        i = v.tensor_mul(attnb[:, n * 512:(n + 1) * 512],
                                         expb[:, n * 512:(n + 1) * 512], dnm[:, :])
                    dv.inc(i)
                b.on("vector", v_body)

            with ExitStack() as sctx:
                pb2 = [sctx.enter_context(nc.psum_tensor(f"{px}pb2_{c}_{i}", [128, 512], F32)) for i in range(2)]
                for n in range(3):
                    for t8 in range(8):
                        pp = pb2[(n * 8 + t8) % 2]

                        def t_body(t, c=c, n=n, t8=t8, pp=pp):
                            t.wait(dv)
                            pe.inc(t.matmul(pp[:, :], bbc[:, t8 * 128:(t8 + 1) * 128],
                                            attnb[:, n * 512:(n + 1) * 512], start=True, stop=True))
                        b.on("tensor", t_body)

                        def v_body(v, c=c, n=n, t8=t8, pp=pp):
                            v.wait(pe)
                            vsl = vc[:, t8 * 514 + n:t8 * 514 + n + 512]
                            if n == 0:
                                dv.inc(v.tensor_mul(acc[:, t8 * 512:(t8 + 1) * 512], pp[:, :], vsl))
                            else:
                                dv.inc(v.tensor_mul(tmpb[:, :], pp[:, :], vsl))
                                v.wait(dv)
                                dst = acc if n == 1 else attnoT
                                dv.inc(v.tensor_add(dst[:, t8 * 512:(t8 + 1) * 512],
                                                    acc[:, t8 * 512:(t8 + 1) * 512], tmpb[:, :]))
                        b.on("vector", v_body)

            with ExitStack() as sctx:
                pyo = [sctx.enter_context(nc.psum_tensor(f"{px}pyo_{c}_{i}", [128, 512], F32)) for i in range(2)]
                for m in range(8):
                    def t_body(t, c=c, m=m):
                        t.wait(dv)
                        t.wait(sc)
                        for it in range(8):
                            pe.inc(t.matmul(pyo[m % 2][:, :], WT["o"][:, it * 1024 + m * 128:it * 1024 + (m + 1) * 128],
                                            attnoT[:, it * 512:(it + 1) * 512],
                                            start=(it == 0), stop=(it == 7)))
                    b.on("tensor", t_body)

                    def sc_body(s, c=c, m=m):
                        s.wait(pe)
                        sc.inc(s.activation(yT[:, m * 512:(m + 1) * 512], pyo[m % 2][:, :], AF.Identity,
                                            bias=bcol["o"][:, m:m + 1]))
                    b.on("scalar", sc_body)

            with ExitStack() as sctx:
                pxt = [sctx.enter_context(nc.psum_tensor(f"{px}pxt_{c}_{i}", [128, 1024], F32)) for i in range(2)]
                for st in range(4):
                    gt = c * 4 + st
                    xb = x2t[st % 2]
                    ob = outt[st % 2]

                    def s_body(s, c=c, st=st, xb=xb):
                        s.wait(dv)
                        sl.inc(s.dma_start(out=xb[:, :], in_=x2[(c * 4 + st) * 128:(c * 4 + st + 1) * 128, :]), 16)
                    b.on("sync", s_body)

                    def t_body(t, c=c, st=st):
                        t.wait(sc)
                        t.wait(dv)
                        for m in range(8):
                            pe.inc(t.transpose(pxt[st % 2][:, m * 128:(m + 1) * 128],
                                               yT[:, m * 512 + st * 128:m * 512 + (st + 1) * 128],
                                               identf[:, :]))
                    b.on("tensor", t_body)

                    def v_body(v, c=c, st=st, xb=xb, ob=ob, sob_=sob[st % 2]):
                        v.wait(pe)
                        v.wait(sl)
                        v.wait(sob_)
                        dv.inc(v.tensor_add(ob[:, :], pxt[st % 2][:, :], xb[:, :]))
                    b.on("vector", v_body)

                    def s2_body(s, c=c, st=st, ob=ob, sob_=sob[st % 2], gt=gt):
                        s.wait(dv)
                        sob_.inc(s.dma_start(out=out[gt * 128:(gt + 1) * 128, :], in_=ob[:, :]), 16)
                    b.on("sync", s2_body)

        def fin(s):
            s.wait(sob[0])
            s.wait(sob[1])
        b.on("sync", fin)
    return nc



def build_launch1(nc):
    with ExitStack() as octx:
        block = octx.enter_context(nc.Block())
        b = B(nc, octx, block)
        env = {"block": block, "b": b}
        x1ap = nc.dram_tensor("x1", [S, D], F32, kind="ExternalOutput").ap()
        build_attn0(nc, dict(x="x", wq="wq", wk="wk", wv="wv", wo="wo", out=x1ap), env=env)
        b.barrier()
        build_moe(nc, dict(x=x1ap, gate_w="gate_w", lb_bias="lb_bias", out="part"),
                  "partials", env=env)
        b.barrier()
    return nc


def build_launch2(nc):
    with ExitStack() as octx:
        block = octx.enter_context(nc.Block())
        b = B(nc, octx, block)
        env = {"block": block, "b": b}
        x2ap = nc.dram_tensor("x2i", [S, D], F32, kind="Internal").ap()
        x3ap = nc.dram_tensor("x3i", [S, D], F32, kind="Internal").ap()
        build_moe(nc, dict(x="x1", gate_w="gate_w", lb_bias="lb_bias", egw="egw",
                           euw="euw", edw="edw", sgw="sgw", suw="suw", sdw="sdw",
                           Sstack="Sstack", Cstack="Cstack", out=x2ap), "full", env=env)
        b.barrier()
        namesD = dict(x=x2ap, out=x3ap)
        for w in "pqkvo":
            namesD["W" + w] = "W" + w
            namesD["b" + w] = "b" + w
        build_attn1(nc, namesD, env=env)
        b.barrier()
        build_ffn1(nc, dict(x=x3ap, k0="k0", k1="k1", k2="k2", s0="s0", s1="s1",
                            s2="s2", out="out"), env=env)
        b.barrier()
    return nc



NCORES = 8


def kernel(**inputs):
    ii = {k: np.ascontiguousarray(v) for k, v in inputs.items()}
    x = ii["L0_attn_in_features"]
    cores = list(range(NCORES))
    lbr = ii["L0_ffn_lb_bias"].reshape(1, 8)

    nc1 = bass.Bass("TRN2", target_bir_lowering=False, debug=False)
    build_launch1(nc1)
    maps1 = [dict(x=x[i], wq=ii["L0_attn_q_proj_weight"], wk=ii["L0_attn_k_proj_weight"],
                  wv=ii["L0_attn_v_proj_weight"], wo=ii["L0_attn_o_proj_weight"],
                  gate_w=ii["L0_ffn_gate_w"], lb_bias=lbr) for i in cores]
    res1 = run_bass_kernel_spmd(nc1, maps1, core_ids=cores)
    x1 = [res1.results[i]["x1"] for i in cores]
    Sg = np.zeros((8, 8), np.float32)
    Cg = np.zeros((8, 8), np.float32)
    for i in cores:
        p = res1.results[i]["part"][:, 0]
        Sg[:, i] = p[0:8]
        Cg[:, i] = p[8:16]

    s0r = np.asarray(ii["L1_ffn_s0"], np.float32).reshape(1, 1)
    s1r = np.asarray(ii["L1_ffn_s1"], np.float32).reshape(1, 1)
    s2r = np.asarray(ii["L1_ffn_s2"], np.float32).reshape(1, 1)
    nc2 = bass.Bass("TRN2", target_bir_lowering=False, debug=False)
    build_launch2(nc2)
    maps2 = []
    for i in cores:
        m = dict(x1=x1[i], gate_w=ii["L0_ffn_gate_w"], lb_bias=lbr,
                 egw=ii["L0_ffn_expert_gate_w"], euw=ii["L0_ffn_expert_up_w"],
                 edw=ii["L0_ffn_expert_down_w"], sgw=ii["L0_ffn_shared_gate_w"],
                 suw=ii["L0_ffn_shared_up_w"], sdw=ii["L0_ffn_shared_down_w"],
                 Sstack=Sg, Cstack=Cg,
                 k0=ii["L1_ffn_k0"].view(np.uint8), k1=ii["L1_ffn_k1"].view(np.uint8),
                 k2=ii["L1_ffn_k2"].view(np.uint8), s0=s0r, s1=s1r, s2=s2r)
        for w in "pqkvo":
            m["W" + w] = ii["L1_attn_W" + w]
            m["b" + w] = ii["L1_attn_b" + w].reshape(1, D)
        maps2.append(m)
    res2 = run_bass_kernel_spmd(nc2, maps2, core_ids=cores)
    return np.stack([res2.results[i]["out"] for i in cores]).astype(np.float32)
